# version 06
# baseline (speedup 1.0000x reference)
import numpy as np
from contextlib import ExitStack

import concourse.bass as bass
import concourse.mybir as mybir
from concourse.bass_utils import run_bass_kernel_spmd

F32 = mybir.dt.float32
F16 = mybir.dt.float16
ALU = mybir.AluOpType
AF = mybir.ActivationFunctionType
AX = mybir.AxisListType

D = 1024
DFF = 4096
T = 2048
NKD = 8
NMF = 32
NTB = 16
NG = 8
C_MAGIC = 12582912.0
EPS = 1e-5
KB = 1024

import os
_STAGES = os.environ.get("MK_STAGES", "111")
_FFN_CUT = os.environ.get("MK_FFN_CUT", "full")
_DEBUG = os.environ.get("MK_DEBUG", "0") == "1"
DO_FFN1 = _STAGES[0] == "1"
DO_ATTN = _STAGES[1] == "1"
DO_FFN2 = _STAGES[2] == "1"

RG2 = [[0, 1], [2, 3], [4, 5], [6, 7]]
RG8 = [list(range(8))]


class _FakeIns:
    def then_inc(self, sem, n):
        return self


class _FakeEng:
    def __getattr__(self, name):
        if name in ("wait_ge",):
            return lambda *a, **k: None
        return lambda *a, **k: _FakeIns()


class _FakeBlock:
    def _run(self, f):
        f(_FakeEng())
    def sync(self, f): self._run(f)
    def vector(self, f): self._run(f)
    def scalar(self, f): self._run(f)
    def tensor(self, f): self._run(f)
    def gpsimd(self, f): self._run(f)


def build_nc():
    nc = bass.Bass("TRN2", target_bir_lowering=False, detect_race_conditions=False)

    xT = nc.dram_tensor("xT", [D, T], F32, kind="ExternalInput").ap()
    w_in_r = nc.dram_tensor("w_in_r", [NG, D, 512], F32, kind="ExternalInput").ap()
    w_out_d = nc.dram_tensor("w_out_d", [DFF, D], F32, kind="ExternalInput").ap()
    w_qkv_r = nc.dram_tensor("w_qkv_r", [6, D, 512], F32, kind="ExternalInput").ap()
    b_in_d = nc.dram_tensor("b_in_d", [128, 32], F32, kind="ExternalInput").ap()
    b_outh_d = nc.dram_tensor("b_outh_d", [128, 8], F32, kind="ExternalInput").ap()
    roff_d = nc.dram_tensor("roff_d", [128, 1], F32, kind="ExternalInput").ap()
    ident16_d = nc.dram_tensor("ident16_d", [128, 128], F16, kind="ExternalInput").ap()
    ident32_d = nc.dram_tensor("ident32_d", [128, 128], F32, kind="ExternalInput").ap()
    maskc_d = nc.dram_tensor("maskc_d", [128, 1024], F32, kind="ExternalInput").ap()
    outT = nc.dram_tensor("outT", [D, T], F32, kind="ExternalOutput").ap()
    if _DEBUG:
        dbg_scal = nc.dram_tensor("dbg_scal", [128, 16], F32, kind="ExternalOutput").ap()
        dbg_row = nc.dram_tensor("dbg_row", [128, 64], F32, kind="ExternalOutput").ap()
        dbg_xn = nc.dram_tensor("dbg_xn", [128, T], F16, kind="ExternalOutput").ap()
        dbg_hid = nc.dram_tensor("dbg_hid", [128, 512], F16, kind="ExternalOutput").ap()
        dbg_sr = nc.dram_tensor("dbg_sr", [1, 128], F32, kind="ExternalOutput").ap()
        dbg_hd = nc.dram_tensor("dbg_hd", [32, 128, 512], F16, kind="ExternalOutput").ap()
        dbg_sbc_d = nc.dram_tensor("dbg_sbc_d", [128, 512], F32, kind="ExternalOutput").ap()
        dbg_ci = nc.dram_tensor("dbg_ci", [128], F32, kind="ExternalOutput").ap()
        dbg_co = nc.dram_tensor("dbg_co", [128], F32, kind="ExternalOutput").ap()

    hidden = nc.dram_tensor("hidden", [4, NMF, 128, 512], F16).ap()
    w_in_int_d = nc.dram_tensor("w_in_int_d", [NG, 128, 4096], F16).ap()
    w_out_int_d = nc.dram_tensor("w_out_int_d", [NMF, 128, D], F16).ap()
    cc_am_in = [nc.dram_tensor(f"cc_am_in{i}", [128], F32) for i in range(4)]
    cc_am_out = [nc.dram_tensor(f"cc_am_out{i}", [128], F32) for i in range(4)]
    cc_k_in = nc.dram_tensor("cc_k_in", [D, T], F16)
    cc_k_out = nc.dram_tensor("cc_k_out", [2 * D, T], F16)
    cc_v_in = nc.dram_tensor("cc_v_in", [T, D], F16)
    cc_vo = nc.dram_tensor("cc_vo", [2 * T, D], F16)

    SB_BASE = 16512

    def at(name, shape, dtype, off):
        assert SB_BASE + off + shape[1] * mybir.dt.size(dtype) <= 229344, name
        return nc.alloc_sbuf_tensor_at(name, shape, dtype, offset=SB_BASE + off).ap()

    x_fm = [at(f"x_fm{k}", [128, T], F32, k * 8 * KB) for k in range(NKD)]
    mean_b = at("mean_b", [128, T], F32, 64 * KB)
    rstd_b = at("rstd_b", [128, T], F32, 72 * KB)
    M0 = 80 * KB
    sb_bin = at("sb_bin", [128, 32], F32, M0 + 0)
    sb_bouth = at("sb_bouth", [128, 8], F32, M0 + 128)
    sb_roff = at("sb_roff", [128, 1], F32, M0 + 160)
    sb_const = at("sb_const", [128, 16], F32, M0 + 192)
    ones_f16 = at("ones_f16", [128, 128], F16, M0 + 256)
    ones_f32k1 = at("ones_f32k1", [1, 128], F32, M0 + 512)
    ones_f32_128 = at("ones_f32_128", [128, 128], F32, M0 + 1024)
    ident_f16 = at("ident_f16", [128, 128], F16, M0 + 1536)
    ident_f32 = at("ident_f32", [128, 128], F32, M0 + 1792)
    masks = at("masks", [128, 1024], F32, M0 + 3328)
    invs_all = at("invs_all", [128, 32], F32, M0 + 7424)
    woamax = at("woamax", [128, 32], F32, M0 + 7552)
    comb = at("comb", [128, 32], F32, M0 + 7680)
    samax = at("samax", [128, 1], F32, M0 + 7808)
    stmp1 = at("stmp1", [128, 1], F32, M0 + 7840)
    stmp2 = at("stmp2", [128, 1], F32, M0 + 7872)
    amax_bc = at("amax_bc", [128, 1], F32, M0 + 7904)
    s_q = at("s_q", [128, 1], F32, M0 + 7936)
    inv_q = at("inv_q", [128, 1], F32, M0 + 7968)
    s_q2 = at("s_q2", [128, 1], F32, M0 + 8000)
    sone = at("sone", [1, 1], F32, M0 + 8032)
    srow128 = at("srow128", [1, 128], F32, M0 + 8064)
    rl = at("rl", [128, 16], F32, M0 + 8576)
    wamax4 = at("wamax4", [128, 4], F32, M0 + 8640)
    sfm4 = at("sfm4", [128, 4], F32, M0 + 8672)
    diag = at("diag", [128, 512], F32, M0 + 8704)
    tmpq = at("tmpq", [128, 1024], F32, M0 + 10752)
    hsc = at("hsc", [128, 128], F32, M0 + 14848)
    dbg_sbc_t = at("dbg_sbc_t", [128, 512], F32, M0 + 15360)
    vdrain = at("vdrain", [128, 256], F32, M0 + 17408)
    AB = 98 * KB
    xn = [at(f"xn{k}", [128, T], F16, AB + k * 4 * KB) for k in range(NKD)]
    wg = [at(f"wg{i}", [128, 4096], F32, AB + 32 * KB + i * 16 * KB) for i in range(2)]
    w_int_g = [at(f"w_int_g{i}", [128, 4096], F16, AB + 64 * KB + i * 8 * KB)
               for i in range(2)]
    hidb = [at(f"hidb{i}", [128, 512], F16, AB + 80 * KB + i * KB) for i in range(4)]
    wog = at("wog", [128, 4096], F32, AB + 84 * KB)
    lnxb = [at(f"lnxb{i}", [128, 1024], F16, AB + 100 * KB + i * 2 * KB)
            for i in range(2)]
    lnsq = [at(f"lnsq{i}", [128, 1024], F16, AB + 104 * KB + i * 2 * KB)
            for i in range(2)]
    harena = at("harena", [128, NMF * 512], F16, AB)
    wo_res = [at(f"wo_res{k}", [128, D], F16, AB + 32 * KB + k * 2 * KB)
              for k in range(NMF)]
    res_t = [at(f"res_t{i}", [128, 512], F32, AB + 96 * KB + i * 2 * KB)
             for i in range(2)]
    q_fm = [at(f"q_fm{k}", [128, T], F16, AB + 32 * KB + k * 4 * KB)
            for k in range(NKD)]
    qkv_ev = [at(f"qkv_ev{i}", [128, 512], F16, AB + 64 * KB + i * KB)
              for i in range(4)]
    wqkv_g = at("wqkv_g", [128, 4096], F16, AB + 68 * KB)
    srows = [at(f"srows{i}", [128, 4096], F16, AB + i * 8 * KB) for i in range(2)]
    prows = [at(f"prows{i}", [128, 4096], F16, AB + 16 * KB + i * 8 * KB)
             for i in range(2)]
    ktile = [at(f"ktile{i}", [128, 4096], F16, AB + 64 * KB + i * 8 * KB)
             for i in range(2)]
    vtile = [at(f"vtile{i}", [128, D], F16, AB + 80 * KB + i * 2 * KB)
             for i in range(2)]
    pt_sb = [at(f"pt_sb{i}", [128, 1024], F16, AB + 84 * KB + i * 2 * KB)
             for i in range(2)]
    av_t = [at(f"av_t{i}", [128, D], F32, AB + 88 * KB + i * 4 * KB)
            for i in range(2)]

    with ExitStack() as stack:
        E = stack.enter_context
        psA = E(nc.psum_tensor("psA", [128, 2048], F32))
        psB = E(nc.psum_tensor("psB", [128, 1024], F32))
        psPT = [E(nc.psum_tensor(f"psPT{i}", [128, 1024], F16)) for i in range(2)]
        slots = [psA[:, i * 512:(i + 1) * 512] for i in range(4)]

        sems = {n: E(nc.semaphore(n)) for n in ["dl", "dg", "cc", "tp", "vp", "sp", "gp"]}

        st = {"dry": True, "cnt": None, "ev": {}, "evfinal": None}

        def inc(ins, s, n=1, e=None):
            st["cnt"][s] += n
            if not st["dry"]:
                ins.then_inc(sems[s], n)
            if e is not None:
                if st["dry"]:
                    st["ev"][e] = (s, st["cnt"][s])
                else:
                    assert st["ev"][e] == (s, st["cnt"][s]), \
                        f"nondeterministic emission at event {e}"

        def mark(e, s):
            if st["dry"]:
                st["ev"][e] = (s, st["cnt"][s])
            else:
                assert st["ev"][e] == (s, st["cnt"][s]), f"mark drift at {e}"

        def wait(eng, e):
            if st["dry"]:
                return
            s, v = st["ev"][e]
            eng.wait_ge(sems[s], v)

        def vpad(vector):
            return vector.tensor_copy(vdrain[:, 4:5], ident_f32[:, 0:1])

        def emit_all(block):
            @block.sync
            def _(sync):
                for ap_, src in [(sb_bin, b_in_d), (sb_bouth, b_outh_d),
                                 (sb_roff, roff_d), (ident_f16, ident16_d),
                                 (ident_f32, ident32_d), (masks, maskc_d)]:
                    inc(sync.dma_start(out=ap_, in_=src), "dl", 16)
                for k in range(NKD):
                    inc(sync.dma_start(out=x_fm[k],
                                       in_=xT[k * 128:(k + 1) * 128, :]), "dl", 16)
                mark("x_loaded", "dl")

            @block.vector
            def _(vector):
                for col, val in [(0, EPS), (1, -0.5), (2, 1.0 / 1024.0),
                                 (3, 127.0), (4, 1.0 / 127.0), (5, 32767.0),
                                 (6, 1.0 / 32767.0), (7, 0.5), (8, 1.0 / 32.0)]:
                    vector.memset(sb_const[:, col:col + 1], val)
                vector.memset(ones_f16, 1.0)
                vector.memset(ones_f32k1, 1.0)
                vector.memset(ones_f32_128, 1.0)
                wait(vector, "x_loaded")
                i = vector.tensor_scalar_mul(sb_bouth, sb_bouth, sb_const[:, 7:8])
                inc(i, "vp", 1, e="setup_v")

            def emit_ln(tag, quant, pre):
                @block.vector
                def _(vector):
                    for e in pre:
                        wait(vector, e)
                    wait(vector, "setup_v")
                    for h in range(2):
                        for k in range(NKD):
                            kk = h * NKD + k
                            if kk >= 2:
                                wait(vector, f"{tag}_mm_{kk - 2}")
                            sl = slice(h * 1024, (h + 1) * 1024)
                            i = vector.tensor_copy(lnxb[k % 2][:], x_fm[k][:, sl])
                            inc(i, "vp", 1, e=f"{tag}_xb_{kk}")
                        wait(vector, f"{tag}_stats_{h}")
                        for sidx in range(2):
                            gsl = slice(h * 1024 + sidx * 512,
                                        h * 1024 + (sidx + 1) * 512)
                            vector.tensor_scalar_mul(mean_b[:, gsl], slots[sidx],
                                                     sb_const[:, 2:3])
                            vector.tensor_tensor(rstd_b[:, gsl], mean_b[:, gsl],
                                                 mean_b[:, gsl], ALU.mult)
                            i = vector.scalar_tensor_tensor(
                                rstd_b[:, gsl], slots[2 + sidx], sb_const[:, 2:3],
                                rstd_b[:, gsl], ALU.mult, ALU.subtract)
                            inc(i, "vp", 1, e=f"{tag}_var_{h}_{sidx}")

                @block.scalar
                def _(scalar):
                    for e in pre:
                        wait(scalar, e)
                    for h in range(2):
                        for k in range(NKD):
                            kk = h * NKD + k
                            if kk >= 2:
                                wait(scalar, f"{tag}_mm_{kk - 2}")
                            sl = slice(h * 1024, (h + 1) * 1024)
                            i = scalar.activation(lnsq[k % 2][:], x_fm[k][:, sl],
                                                  AF.Square)
                            inc(i, "sp", 1, e=f"{tag}_sq_{kk}")

                @block.tensor
                def _(tensor):
                    for e in pre:
                        wait(tensor, e)
                    for h in range(2):
                        if h > 0:
                            wait(tensor, f"{tag}_var_0_1")
                        for k in range(NKD):
                            kk = h * NKD + k
                            wait(tensor, f"{tag}_xb_{kk}")
                            wait(tensor, f"{tag}_sq_{kk}")
                            for s in range(2):
                                tsl = slice(s * 512, (s + 1) * 512)
                                tensor.matmul(slots[s], ones_f16,
                                              lnxb[k % 2][:, tsl],
                                              start=(k == 0), stop=(k == NKD - 1))
                                i = tensor.matmul(slots[2 + s], ones_f16,
                                                  lnsq[k % 2][:, tsl],
                                                  start=(k == 0), stop=(k == NKD - 1))
                            inc(i, "tp", 1, e=f"{tag}_mm_{kk}")
                        mark(f"{tag}_stats_{h}", "tp")

                @block.scalar
                def _(scalar):
                    for h in range(2):
                        wait(scalar, f"{tag}_var_{h}_1")
                        gsl = slice(h * 1024, (h + 1) * 1024)
                        scalar.activation(rstd_b[:, gsl], rstd_b[:, gsl], AF.Ln,
                                          bias=sb_const[:, 0:1], scale=1.0)
                        i = scalar.activation(rstd_b[:, gsl], rstd_b[:, gsl],
                                              AF.Exp, bias=0.0,
                                              scale=sb_const[:, 1:2])
                        inc(i, "sp", 1, e=f"{tag}_rstd_{h}")

                @block.vector
                def _(vector):
                    wait(vector, f"{tag}_rstd_0")
                    wait(vector, f"{tag}_rstd_1")
                    for k in range(NKD):
                        vector.tensor_tensor(xn[k], x_fm[k], mean_b, ALU.subtract)
                        i = vector.tensor_tensor(xn[k], xn[k], rstd_b, ALU.mult)
                        inc(i, "vp", 1, e=f"{tag}_xn_{k}")
                        if quant:
                            if k == 0:
                                i = vector.tensor_reduce(
                                    samax, xn[k], AX.X, ALU.max,
                                    apply_absolute_value=True)
                            else:
                                vector.tensor_reduce(stmp1, xn[k], AX.X, ALU.max,
                                                     apply_absolute_value=True)
                                i = vector.tensor_max(samax, samax, stmp1)
                    if quant:
                        inc(i, "vp", 1, e=f"{tag}_amax_local")
                    mark(f"{tag}_xn_all", "vp")

            ar_no = [0]

            def emit_amax_ar(tag, bc_ps):
                idx = ar_no[0]
                ar_no[0] += 1

                @block.gpsimd
                def _(gpsimd):
                    wait(gpsimd, f"{tag}_amax_local")
                    i = gpsimd.dma_start(out=cc_am_in[idx].ap(), in_=samax)
                    inc(i, "dg", 16, e=f"{tag}_arin")
                    wait(gpsimd, f"{tag}_arin")
                    i = gpsimd.collective_compute(
                        "AllReduce", ALU.max, replica_groups=RG8,
                        ins=[cc_am_in[idx].ap()], outs=[cc_am_out[idx].ap()])
                    inc(i, "cc", 1, e=f"{tag}_ar")
                    wait(gpsimd, f"{tag}_ar")
                    i = gpsimd.dma_start(out=srow128, in_=cc_am_out[idx].ap())
                    inc(i, "dg", 16, e=f"{tag}_arrow")

                @block.vector
                def _(vector):
                    wait(vector, f"{tag}_arrow")
                    vector.tensor_reduce(sone, srow128, AX.X, ALU.max)
                    i = vector.tensor_copy(vdrain[0:1, 0:1], sone)
                    inc(i, "vp", 1, e=f"{tag}_sone")

                @block.tensor
                def _(tensor):
                    wait(tensor, f"{tag}_sone")
                    i = tensor.matmul(bc_ps, ones_f32k1, sone, start=True,
                                      stop=True)
                    inc(i, "tp", 1, e=f"{tag}_bcmm")

                @block.scalar
                def _(scalar):
                    wait(scalar, f"{tag}_bcmm")
                    i = scalar.copy(amax_bc, bc_ps)
                    inc(i, "sp", 1, e=f"{tag}_amax_bc")

            def emit_qscales(tag, s_t, with_comb):
                @block.vector
                def _(vector):
                    wait(vector, f"{tag}_amax_bc")
                    vector.reciprocal(stmp2, amax_bc)
                    vector.tensor_scalar_mul(inv_q, amax_bc, sb_const[:, 6:7])
                    vpad(vector)
                    vpad(vector)
                    vector.tensor_scalar_mul(s_t, stmp2, sb_const[:, 5:6])
                    vpad(vector)
                    i = vpad(vector)
                    inc(i, "vp", 1, e=f"{tag}_scales")
                    if with_comb:
                        vpad(vector)
                        vector.tensor_scalar_mul(stmp2, inv_q, sb_const[:, 4:5])
                        vpad(vector)
                        vpad(vector)
                        i = vector.tensor_scalar_mul(comb, woamax, stmp2)
                        inc(i, "vp", 1, e=f"{tag}_comb")

            def emit_ffn(ci, pre):
                tag = f"f{ci}"
                ltag = f"lnf{ci}"
                first = ci == 0
                emit_ln(ltag, quant=True, pre=pre)
                if _FFN_CUT == "ln":
                    st["ev"][f"{tag}_done"] = st["ev"][f"{ltag}_xn_all"]
                    return
                emit_amax_ar(ltag, psB[:, 512:513])
                emit_qscales(ltag, s_q, with_comb=False)
                if _FFN_CUT == "ar":
                    st["ev"][f"{tag}_done"] = st["ev"][f"{ltag}_scales"]
                    return

                @block.vector
                def _(vector):
                    wait(vector, f"{ltag}_scales")
                    for k in range(NKD):
                        vector.tensor_scalar(mean_b, xn[k], s_q, C_MAGIC,
                                             ALU.mult, ALU.add)
                        i = vector.tensor_scalar(xn[k], mean_b, -C_MAGIC, inv_q,
                                                 ALU.add, ALU.mult)
                        inc(i, "vp", 1, e=f"{tag}_xsim_{k}")
                    mark(f"{tag}_xsim_all", "vp")
                if _FFN_CUT == "xsim":
                    st["ev"][f"{tag}_done"] = st["ev"][f"{tag}_xsim_all"]
                    return

                @block.sync
                def _(sync):
                    for g in range(NG):
                        if first:
                            if g >= 2:
                                wait(sync, f"{tag}_wgdone_{g - 2}")
                            elif g == 0:
                                for e in pre:
                                    wait(sync, e)
                            i = sync.dma_start(
                                out=wg[g % 2].rearrange("p (k f) -> p k f", f=512),
                                in_=w_in_r[g].rearrange("(k p) f -> p k f", p=128))
                            inc(i, "dl", 16, e=f"{tag}_wgld_{g}")
                        else:
                            if g >= 2:
                                wait(sync, f"{tag}_mm1_g{g - 2}")
                            elif g == 0:
                                for e in pre:
                                    wait(sync, e)
                            i = sync.dma_start(out=w_int_g[g % 2],
                                               in_=w_in_int_d[g])
                            inc(i, "dl", 16, e=f"{tag}_wintq_{g}")
                        if g > 0:
                            gp = g - 1
                            for ml in range(4):
                                m = gp * 4 + ml
                                for t in range(4):
                                    idx = (gp * 4 + ml) * 4 + t
                                    wait(sync, f"{tag}_gelu_{idx}")
                                    i = sync.dma_start(out=hidden[t, m],
                                                       in_=hidb[idx % 4])
                                    inc(i, "dl", 16, e=f"{tag}_hidw_{idx}")
                            if first:
                                wait(sync, f"{tag}_wintq_{gp}")
                                wait(sync, f"{tag}_mm1_g{gp}")
                                i = sync.dma_start(out=w_in_int_d[gp],
                                                   in_=w_int_g[gp % 2])
                                inc(i, "dl", 16, e=f"{tag}_wintw_{gp}")
                                for kk in range(4):
                                    m = gp * 4 + kk
                                    wait(sync, f"{tag}_woq_{m}")
                                    i = sync.dma_start(out=w_out_int_d[m],
                                                       in_=lnxb[kk % 2])
                                    inc(i, "dl", 16, e=f"{tag}_wow_{m}")
                        if first:
                            if g > 0:
                                wait(sync, f"{tag}_wodone_{g - 1}")
                            else:
                                wait(sync, f"{ltag}_xn_all")
                            i = sync.dma_start(
                                out=wog.rearrange("p (k f) -> p k f", f=D),
                                in_=w_out_d[g * 512:(g + 1) * 512, :].rearrange(
                                    "(k p) f -> p k f", p=128))
                            inc(i, "dl", 16, e=f"{tag}_wold_{g}")
                    gp = NG - 1
                    for ml in range(4):
                        m = gp * 4 + ml
                        for t in range(4):
                            idx = (gp * 4 + ml) * 4 + t
                            wait(sync, f"{tag}_gelu_{idx}")
                            i = sync.dma_start(out=hidden[t, m], in_=hidb[idx % 4])
                            inc(i, "dl", 16, e=f"{tag}_hidw_{idx}")
                    mark(f"{tag}_hidw_all", "dl")
                    if first:
                        wait(sync, f"{tag}_wintq_{gp}")
                        wait(sync, f"{tag}_mm1_g{gp}")
                        i = sync.dma_start(out=w_in_int_d[gp], in_=w_int_g[gp % 2])
                        inc(i, "dl", 16, e=f"{tag}_wintw_{gp}")
                        for kk in range(4):
                            m = gp * 4 + kk
                            wait(sync, f"{tag}_woq_{m}")
                            i = sync.dma_start(out=w_out_int_d[m], in_=lnxb[kk % 2])
                            inc(i, "dl", 16, e=f"{tag}_wow_{m}")

                @block.vector
                def _(vector):
                    if first:
                        wait(vector, f"{ltag}_xn_all")
                    for g in range(NG):
                        if first:
                            wait(vector, f"{tag}_wgld_{g}")
                            w = wg[g % 2]
                            cm = rstd_b[:, 0:512]
                            cmn = rstd_b[:, 1024:1536]
                            vector.tensor_tensor(cm, w[:, 0:512], w[:, 512:1024],
                                                 ALU.max)
                            vector.tensor_tensor(cmn, w[:, 0:512], w[:, 512:1024],
                                                 ALU.min)
                            for k in range(2, NKD):
                                vector.tensor_tensor(
                                    cm, cm, w[:, k * 512:(k + 1) * 512], ALU.max)
                                vector.tensor_tensor(
                                    cmn, cmn, w[:, k * 512:(k + 1) * 512], ALU.min)
                            vector.tensor_scalar_mul(cmn, cmn, -1.0)
                            vector.tensor_tensor(cm, cm, cmn, ALU.max)
                            i = vector.tensor_copy(vdrain[:, 0:128],
                                                   cm[:, 384:512])
                            inc(i, "vp", 1, e=f"{tag}_cm_{g}")
                            wait(vector, f"{tag}_cmT_{g}")
                            for c in range(4):
                                vector.tensor_reduce(
                                    wamax4[:, c:c + 1],
                                    psB[:, c * 128:(c + 1) * 128], AX.X, ALU.max)
                            vpad(vector)
                            vpad(vector)
                            vector.reciprocal(sfm4, wamax4)
                            i = vector.tensor_scalar_mul(
                                invs_all[:, g * 4:g * 4 + 4], wamax4,
                                sb_const[:, 4:5])
                            inc(i, "vp", 1, e=f"{tag}_invs_{g}")
                            vpad(vector)
                            vpad(vector)
                            vector.tensor_scalar_mul(sfm4, sfm4, sb_const[:, 3:4])
                            vpad(vector)
                            vpad(vector)
                            for c in range(4):
                                vector.tensor_scalar_mul(
                                    diag[:, c * 128:(c + 1) * 128], ident_f32,
                                    sfm4[:, c:c + 1])
                            i = vector.tensor_copy(vdrain[:, 0:128],
                                                   diag[:, 384:512])
                            inc(i, "vp", 1, e=f"{tag}_diag_{g}")
                            wait(vector, f"{tag}_sbc_{g}")
                            if _DEBUG and g == 1:
                                vector.tensor_copy(dbg_sbc_t[:, 0:128],
                                                   psB[:, 384:512])
                                vector.tensor_copy(dbg_sbc_t[:, 128:256],
                                                   diag[:, 384:512])
                                vector.tensor_copy(dbg_sbc_t[:, 256:260], sfm4)
                            if g >= 2:
                                wait(vector, f"{tag}_mm1_g{g - 2}")
                                wait(vector, f"{tag}_wintw_{g - 2}")
                            for k in range(NKD):
                                ksl = slice(k * 512, (k + 1) * 512)
                                vector.tensor_tensor(rstd_b[:, 512:1024], w[:, ksl],
                                                     psB[:, 0:512], ALU.mult)
                                i = vector.tensor_scalar(
                                    w_int_g[g % 2][:, ksl], rstd_b[:, 512:1024],
                                    C_MAGIC, -C_MAGIC, ALU.add, ALU.add)
                            inc(i, "vp", 1, e=f"{tag}_wintq_{g}")
                            mark(f"{tag}_wgdone_{g}", "vp")
                        for idx in range(g * 16, g * 16 + 16):
                            wait(vector, f"{tag}_gelu_{idx}")
                            i = vector.tensor_reduce(hsc[:, idx:idx + 1],
                                                     hidb[idx % 4], AX.X, ALU.max)
                            inc(i, "vp", 1, e=f"{tag}_hamax_{idx}")
                        if first:
                            og = g
                            wait(vector, f"{tag}_wold_{og}")
                            for kk in range(4):
                                m = og * 4 + kk
                                wsl = wog[:, kk * D:(kk + 1) * D]
                                vector.tensor_reduce(woamax[:, m:m + 1], wsl, AX.X,
                                                     ALU.max,
                                                     apply_absolute_value=True)
                            vpad(vector)
                            vpad(vector)
                            vector.reciprocal(sfm4,
                                              woamax[:, og * 4:og * 4 + 4])
                            vpad(vector)
                            vpad(vector)
                            vector.tensor_scalar_mul(sfm4, sfm4, sb_const[:, 3:4])
                            vpad(vector)
                            vpad(vector)
                            for kk in range(4):
                                m = og * 4 + kk
                                if m >= 2:
                                    wait(vector, f"{tag}_wow_{m - 2}")
                                wsl = wog[:, kk * D:(kk + 1) * D]
                                vector.tensor_scalar(tmpq, wsl, sfm4[:, kk:kk + 1],
                                                     C_MAGIC, ALU.mult, ALU.add)
                                i = vector.tensor_scalar(lnxb[kk % 2], tmpq,
                                                         -C_MAGIC, None, ALU.add)
                                inc(i, "vp", 1, e=f"{tag}_woq_{m}")
                            mark(f"{tag}_wodone_{og}", "vp")
                    i = vector.tensor_reduce(samax, hsc, AX.X, ALU.max)
                    inc(i, "vp", 1)
                    mark(f"{ltag}h_amax_local", "vp")

                @block.tensor
                def _(tensor):
                    wait(tensor, f"{tag}_xsim_all")
                    idx = 0
                    for g in range(NG):
                        if first:
                            wait(tensor, f"{tag}_cm_{g}")
                            if g == 0:
                                wait(tensor, f"{ltag}_var_1_1")
                            else:
                                wait(tensor, f"{tag}_wintq_{g - 1}")
                            for c in range(4):
                                tensor.matmul(psB[:, c * 128:(c + 1) * 128],
                                              rstd_b[:, c * 128:(c + 1) * 128],
                                              ident_f32, start=(c == 0),
                                              stop=(c == 3), is_transpose=True)
                            i = tensor.matmul(psB[:, 512:513], ones_f32k1, sone,
                                              start=True, stop=True)
                            inc(i, "tp", 1, e=f"{tag}_cmT_{g}")
                            wait(tensor, f"{tag}_diag_{g}")
                            wait(tensor, f"{tag}_invs_{g}")
                            tensor.matmul(psB[:, 0:512], ones_f32_128, diag,
                                          start=True, stop=True)
                            i = tensor.matmul(psB[:, 512:513], ones_f32k1, sone,
                                              start=True, stop=True)
                            inc(i, "tp", 1, e=f"{tag}_sbc_{g}")
                        wait(tensor, f"{tag}_wintq_{g}")
                        for ml in range(4):
                            for t in range(4):
                                slot = slots[idx % 4]
                                if idx >= 4:
                                    wait(tensor, f"{tag}_gelu_{idx - 4}")
                                for k in range(NKD):
                                    lhsT = w_int_g[g % 2][
                                        :, k * 512 + ml * 128:k * 512 + (ml + 1) * 128]
                                    i = tensor.matmul(
                                        slot, lhsT, xn[k][:, t * 512:(t + 1) * 512],
                                        start=(k == 0), stop=(k == NKD - 1))
                                inc(i, "tp", 1, e=f"{tag}_mm1_{idx}")
                                idx += 1
                        mark(f"{tag}_mm1_g{g}", "tp")
                    mark(f"{tag}_mm1_all", "tp")

                @block.scalar
                def _(scalar):
                    idx = 0
                    for g in range(NG):
                        for ml in range(4):
                            m = g * 4 + ml
                            for t in range(4):
                                wait(scalar, f"{tag}_mm1_{idx}")
                                if idx >= 4:
                                    wait(scalar, f"{tag}_hidw_{idx - 4}")
                                    wait(scalar, f"{tag}_hamax_{idx - 4}")
                                i = scalar.activation(
                                    hidb[idx % 4], slots[idx % 4], AF.Gelu,
                                    bias=sb_bin[:, m:m + 1],
                                    scale=invs_all[:, m:m + 1])
                                inc(i, "sp", 1, e=f"{tag}_gelu_{idx}")
                                idx += 1

                if _FFN_CUT == "mm1":
                    st["ev"][f"{tag}_done"] = st["ev"][f"{tag}_hidw_all"]
                    return
                htag = f"{ltag}h"
                emit_amax_ar(htag, psB[:, 513:514])
                emit_qscales(htag, s_q2, with_comb=True)
                if _FFN_CUT == "ar2":
                    st["ev"][f"{tag}_done"] = st["ev"][f"{htag}_comb"]
                    return

                @block.sync
                def _(sync):
                    wait(sync, f"{tag}_mm1_all")
                    wait(sync, f"{tag}_hidw_all")
                    wait(sync, f"{ltag}h_amax_local")
                    if first:
                        wait(sync, f"{tag}_wow_{NMF - 1}")
                    for m in range(NMF):
                        i = sync.dma_start(out=wo_res[m], in_=w_out_int_d[m])
                        inc(i, "dl", 16, e=f"{tag}_wo2_{m}")
                    mark(f"{tag}_wo2_all", "dl")
                    for tq in range(4):
                        if tq >= 1:
                            wait(sync, f"{tag}_hfree_{tq - 1}")
                        i = sync.dma_start(
                            out=harena.rearrange("p (k f) -> p k f", f=512),
                            in_=hidden[tq].rearrange("k p f -> p k f"))
                        inc(i, "dl", 16, e=f"{tag}_hrd_{tq}")

                @block.vector
                def _(vector):
                    it = 0
                    for tq in range(4):
                        wait(vector, f"{tag}_hrd_{tq}")
                        wait(vector, f"{htag}_comb")
                        for k in range(NMF):
                            hsl = harena[:, k * 512:(k + 1) * 512]
                            vector.tensor_scalar(tmpq[:, 0:512], hsl, s_q2,
                                                 C_MAGIC, ALU.mult, ALU.add)
                            vector.tensor_scalar(hsl, tmpq[:, 0:512], -C_MAGIC,
                                                 comb[:, k:k + 1], ALU.add,
                                                 ALU.mult)
                            i = vector.tensor_copy(vdrain[:, 0:64],
                                                   hsl[:, 448:512])
                            inc(i, "vp", 1, e=f"{tag}_hq_{tq}_{k}")
                        if _FFN_CUT in ("mm2a", "mm2b"):
                            mark(f"{tag}_hfree_{tq}", "vp")
                            continue
                        tsl = slice(tq * 512, (tq + 1) * 512)
                        for mset in range(2):
                            for m4 in range(4):
                                m = mset * 4 + m4
                                j = it * 4 + m4
                                wait(vector, f"{tag}_resa_{it}_{m4}")
                                i = vector.tensor_tensor(x_fm[m][:, tsl],
                                                         x_fm[m][:, tsl],
                                                         res_t[j % 2], ALU.add)
                                inc(i, "vp", 1, e=f"{tag}_res_{j}")
                            it += 1
                        mark(f"{tag}_hfree_{tq}", "vp")
                    mark(f"{tag}_done", "vp")

                if _FFN_CUT == "mm2a":
                    return

                @block.tensor
                def _(tensor):
                    wait(tensor, f"{tag}_wo2_all")
                    it = 0
                    for tq in range(4):
                        for mset in range(2):
                            for k in range(NMF):
                                wait(tensor, f"{tag}_hq_{tq}_{k}")
                                for m4 in range(4):
                                    m = mset * 4 + m4
                                    if k == 0 and it >= 1 and _FFN_CUT == "full":
                                        wait(tensor, f"{tag}_resa_{it - 1}_{m4}")
                                    i = tensor.matmul(
                                        slots[m4],
                                        wo_res[k][:, m * 128:(m + 1) * 128],
                                        harena[:, k * 512:(k + 1) * 512],
                                        start=(k == 0), stop=(k == NMF - 1))
                                    if k == NMF - 1:
                                        inc(i, "tp", 1, e=f"{tag}_mm2_{it}_{m4}")
                            it += 1
                    mark(f"{tag}_mm2_all", "tp")

                if _FFN_CUT == "mm2b":
                    st["ev"][f"{tag}_done"] = st["ev"][f"{tag}_mm2_all"]
                    return

                @block.scalar
                def _(scalar):
                    it = 0
                    for tq in range(4):
                        for mset in range(2):
                            for m4 in range(4):
                                m = mset * 4 + m4
                                j = it * 4 + m4
                                wait(scalar, f"{tag}_mm2_{it}_{m4}")
                                if j >= 2:
                                    wait(scalar, f"{tag}_res_{j - 2}")
                                i = scalar.activation(res_t[j % 2], slots[m4],
                                                      AF.Identity,
                                                      bias=sb_bouth[:, m:m + 1],
                                                      scale=sb_const[:, 7:8])
                                inc(i, "sp", 1, e=f"{tag}_resa_{it}_{m4}")
                            it += 1

            def emit_qkv():
                tag = "qkv"
                emit_ln("lnq", quant=False, pre=["f0_done"])

                @block.gpsimd
                def _(gpsimd):
                    for g in range(6):
                        if g == 0:
                            wait(gpsimd, "lnq_xn_all")
                        else:
                            wait(gpsimd, f"{tag}_mm_g{g - 1}")
                        i = gpsimd.dma_start(
                            out=wqkv_g.rearrange("p (k f) -> p k f", f=512),
                            in_=w_qkv_r[g].rearrange("(k p) f -> p k f", p=128))
                        inc(i, "dg", 16, e=f"{tag}_wld_{g}")

                @block.tensor
                def _(tensor):
                    idx = 0
                    for g in range(6):
                        wait(tensor, f"{tag}_wld_{g}")
                        if g < 2 or g >= 4:
                            for ml in range(4):
                                for t in range(4):
                                    slot = slots[idx % 4]
                                    if idx >= 4:
                                        wait(tensor, f"{tag}_ev_{idx - 4}")
                                    for k in range(NKD):
                                        lhsT = wqkv_g[:, k * 512 + ml * 128:
                                                      k * 512 + (ml + 1) * 128]
                                        i = tensor.matmul(
                                            slot, lhsT,
                                            xn[k][:, t * 512:(t + 1) * 512],
                                            start=(k == 0), stop=(k == NKD - 1))
                                    inc(i, "tp", 1, e=f"{tag}_p_{idx}")
                                    idx += 1
                        else:
                            for tb in range(NTB):
                                slot = slots[idx % 4]
                                if idx >= 4:
                                    wait(tensor, f"{tag}_ev_{idx - 4}")
                                for k in range(NKD):
                                    i = tensor.matmul(
                                        slot, xn[k][:, tb * 128:(tb + 1) * 128],
                                        wqkv_g[:, k * 512:(k + 1) * 512],
                                        start=(k == 0), stop=(k == NKD - 1))
                                inc(i, "tp", 1, e=f"{tag}_p_{idx}")
                                idx += 1
                        mark(f"{tag}_mm_g{g}", "tp")

                @block.scalar
                def _(scalar):
                    idx = 0
                    for g in range(6):
                        nit = 16
                        for n in range(nit):
                            wait(scalar, f"{tag}_p_{idx}")
                            if g < 4:
                                if idx >= 4:
                                    wait(scalar, f"{tag}_evd_{idx - 4}")
                                i = scalar.copy(qkv_ev[idx % 4], slots[idx % 4])
                            else:
                                ml, t = n // 4, n % 4
                                m = (g - 4) * 4 + ml
                                i = scalar.copy(q_fm[m][:, t * 512:(t + 1) * 512],
                                                slots[idx % 4])
                            inc(i, "sp", 1, e=f"{tag}_ev_{idx}")
                            idx += 1

                @block.sync
                def _(sync):
                    idx = 0
                    for g in range(6):
                        if g < 2:
                            for ml in range(4):
                                m = g * 4 + ml
                                for t in range(4):
                                    wait(sync, f"{tag}_ev_{idx}")
                                    i = sync.dma_start(
                                        out=cc_k_in.ap()[m * 128:(m + 1) * 128,
                                                         t * 512:(t + 1) * 512],
                                        in_=qkv_ev[idx % 4])
                                    inc(i, "dl", 16, e=f"{tag}_evd_{idx}")
                                    idx += 1
                            if g == 1:
                                mark(f"{tag}_kin_done", "dl")
                        elif g < 4:
                            for tb in range(NTB):
                                wait(sync, f"{tag}_ev_{idx}")
                                i = sync.dma_start(
                                    out=cc_v_in.ap()[tb * 128:(tb + 1) * 128,
                                                     (g - 2) * 512:(g - 1) * 512],
                                    in_=qkv_ev[idx % 4])
                                inc(i, "dl", 16, e=f"{tag}_evd_{idx}")
                                idx += 1
                            if g == 3:
                                mark(f"{tag}_vin_done", "dl")
                        else:
                            idx += 16

                @block.gpsimd
                def _(gpsimd):
                    wait(gpsimd, f"{tag}_kin_done")
                    i = gpsimd.collective_compute(
                        "AllGather", ALU.bypass, replica_groups=RG2,
                        ins=[cc_k_in.ap()], outs=[cc_k_out.ap()])
                    inc(i, "cc", 1, e="ag_k")
                    wait(gpsimd, f"{tag}_vin_done")
                    i = gpsimd.collective_compute(
                        "AllGather", ALU.bypass, replica_groups=RG2,
                        ins=[cc_v_in.ap()], outs=[cc_vo.ap()])
                    inc(i, "cc", 1, e="ag_v")

            def emit_attn():
                tag = "at"

                @block.sync
                def _(sync):
                    kbuf = 0
                    vbuf = 0
                    for gi in range(8):
                        for c in range(gi + 1):
                            if kbuf < 2:
                                wait(sync, "ag_k")
                                wait(sync, "qkv_mm_g5")
                                wait(sync, "qkv_evd_63")
                            else:
                                wait(sync, f"{tag}_kused_{kbuf - 2}")
                            for h in range(2):
                                i = sync.dma_start(
                                    out=ktile[kbuf % 2].rearrange(
                                        "p (k f) -> p k f",
                                        f=512)[:, :, h * 256:(h + 1) * 256],
                                    in_=cc_k_out.ap()[h * 1024:(h + 1) * 1024,
                                                      2 * c * 128:(2 * c + 2) * 128]
                                    .rearrange("(k p) f -> p k f", p=128))
                                inc(i, "dl", 16)
                            mark(f"{tag}_kld_{kbuf}", "dl")
                            kbuf += 1
                        for blk in range(4 * (gi + 1)):
                            c, sub = blk // 4, blk % 4
                            h, jl = sub // 2, 2 * c + sub % 2
                            if vbuf < 2:
                                wait(sync, "ag_v")
                            else:
                                wait(sync, f"{tag}_vused_{vbuf - 2}")
                            i = sync.dma_start(
                                out=vtile[vbuf % 2],
                                in_=cc_vo.ap()[h * 2048 + jl * 128:
                                                  h * 2048 + (jl + 1) * 128, :])
                            inc(i, "dl", 16, e=f"{tag}_vld_{vbuf}")
                            vbuf += 1

                @block.tensor
                def _(tensor):
                    kbuf = 0
                    vbuf = 0
                    ptidx = 0
                    for gi in range(8):
                        for c in range(gi + 1):
                            wait(tensor, f"{tag}_kld_{kbuf}")
                            for qbi in range(2):
                                j = 2 * gi + qbi
                                sq = 2 * kbuf + qbi
                                if sq >= 4:
                                    wait(tensor, f"{tag}_sev_{sq - 4}")
                                for k in range(NKD):
                                    i = tensor.matmul(
                                        slots[sq % 4],
                                        q_fm[k][:, j * 128:(j + 1) * 128],
                                        ktile[kbuf % 2][:, k * 512:(k + 1) * 512],
                                        start=(k == 0), stop=(k == NKD - 1))
                                inc(i, "tp", 1, e=f"{tag}_qk_{sq}")
                            mark(f"{tag}_kused_{kbuf}", "tp")
                            kbuf += 1
                        nblk = 4 * (gi + 1)
                        for blk in range(nblk):
                            wait(tensor, f"{tag}_vld_{vbuf}")
                            if ptidx >= 2:
                                wait(tensor, f"{tag}_ptev_{ptidx - 2}")
                            for qbi in range(2):
                                wait(tensor, f"{tag}_p_{gi}_{qbi}")
                                i = tensor.matmul(
                                    psPT[blk % 2][:, qbi * 128:(qbi + 1) * 128],
                                    prows[qbi][:, blk * 128:(blk + 1) * 128],
                                    ident_f16, start=(qbi == 0), stop=(qbi == 1),
                                    is_transpose=True)
                            inc(i, "tp", 1, e=f"{tag}_pt_{ptidx}")
                            wait(tensor, f"{tag}_ptev_{ptidx}")
                            for qbi in range(2):
                                avps = psB[:, 0:1024] if qbi == 0 else psA[:, 1024:2048]
                                for half in range(2):
                                    i = tensor.matmul(
                                        avps[:, half * 512:(half + 1) * 512],
                                        pt_sb[blk % 2][:,
                                                       qbi * 128:(qbi + 1) * 128],
                                        vtile[vbuf % 2][:,
                                                        half * 512:(half + 1) * 512],
                                        start=(blk == 0), stop=(blk == nblk - 1))
                            inc(i, "tp", 1)
                            mark(f"{tag}_vused_{vbuf}", "tp")
                            vbuf += 1
                            ptidx += 1
                        mark(f"{tag}_av_done_{gi}", "tp")
                        for qbi in range(2):
                            wait(tensor, f"{tag}_avev_{gi}_{qbi}")
                            for dk in range(NKD):
                                bank = psB[:, 0:512] if dk < 4 else psB[:, 512:1024]
                                i = tensor.matmul(
                                    bank[:, (dk % 4) * 128:(dk % 4 + 1) * 128],
                                    av_t[qbi][:, dk * 128:(dk + 1) * 128],
                                    ident_f32, start=(dk % 4 == 0),
                                    stop=(dk % 4 == 3), is_transpose=True)
                            inc(i, "tp", 1, e=f"{tag}_avT_{gi}_{qbi}")
                            wait(tensor, f"{tag}_resadd_{gi}_{qbi}")
                    mark(f"{tag}_done_t", "tp")

                @block.scalar
                def _(scalar):
                    kbuf = 0
                    for gi in range(8):
                        for c in range(gi + 1):
                            for qbi in range(2):
                                sq = 2 * kbuf + qbi
                                wait(scalar, f"{tag}_qk_{sq}")
                                if sq == 0:
                                    wait(scalar, "qkv_mm_g5")
                                i = scalar.activation(
                                    srows[qbi][:, c * 512:(c + 1) * 512],
                                    slots[sq % 4], AF.Identity, bias=0.0,
                                    scale=sb_const[:, 8:9])
                                inc(i, "sp", 1, e=f"{tag}_sev_{sq}")
                            kbuf += 1
                        for qbi in range(2):
                            wait(scalar, f"{tag}_negmax_{gi}_{qbi}")
                            i = scalar.activation(
                                prows[qbi][:, 0:(gi + 1) * 512],
                                srows[qbi][:, 0:(gi + 1) * 512], AF.Exp,
                                bias=(stmp1 if qbi == 0 else stmp2), scale=1.0)
                            inc(i, "sp", 1, e=f"{tag}_exp_{gi}_{qbi}")
                        for qbi in range(2):
                            j = 2 * gi + qbi
                            wait(scalar, f"{tag}_av_done_{gi}")
                            wait(scalar, f"{tag}_rl_{gi}_{qbi}")
                            avps = psB[:, 0:1024] if qbi == 0 else psA[:, 1024:2048]
                            i = scalar.activation(av_t[qbi], avps, AF.Identity,
                                                  bias=0.0, scale=rl[:, j:j + 1])
                            inc(i, "sp", 1, e=f"{tag}_avev_{gi}_{qbi}")

                @block.vector
                def _(vector):
                    ptidx = 0
                    for gi in range(8):
                        last_kbuf = (gi + 1) * (gi + 2) // 2 - 1
                        for qbi in range(2):
                            j = 2 * gi + qbi
                            wait(vector, f"{tag}_sev_{2 * last_kbuf + qbi}")
                            vector.tensor_tensor(
                                srows[qbi][:, gi * 512:(gi + 1) * 512],
                                srows[qbi][:, gi * 512:(gi + 1) * 512],
                                masks[:, qbi * 512:(qbi + 1) * 512], ALU.add)
                            vector.tensor_reduce(
                                (stmp1 if qbi == 0 else stmp2),
                                srows[qbi][:, 0:(gi + 1) * 512], AX.X, ALU.max,
                                negate=True)
                            i = vector.tensor_copy(
                                vdrain[:, 0:1], stmp1 if qbi == 0 else stmp2)
                            inc(i, "vp", 1, e=f"{tag}_negmax_{gi}_{qbi}")
                            wait(vector, f"{tag}_exp_{gi}_{qbi}")
                            vector.tensor_reduce(samax,
                                                 prows[qbi][:, 0:(gi + 1) * 512],
                                                 AX.X, ALU.add)
                            vpad(vector)
                            vpad(vector)
                            vector.reciprocal(rl[:, j:j + 1], samax)
                            vpad(vector)
                            i = vpad(vector)
                            inc(i, "vp", 1, e=f"{tag}_rl_{gi}_{qbi}")
                            mark(f"{tag}_p_{gi}_{qbi}", "vp")
                        for blk in range(4 * (gi + 1)):
                            wait(vector, f"{tag}_pt_{ptidx}")
                            vector.tensor_copy(pt_sb[blk % 2][:, 0:256],
                                               psPT[blk % 2][:, 0:256])
                            i = vector.tensor_copy(
                                vdrain[:, 0:64], pt_sb[blk % 2][:, 192:256])
                            inc(i, "vp", 1, e=f"{tag}_ptev_{ptidx}")
                            ptidx += 1
                        for qbi in range(2):
                            j = 2 * gi + qbi
                            wait(vector, f"{tag}_avT_{gi}_{qbi}")
                            for dk in range(NKD):
                                bank = psB[:, 0:512] if dk < 4 else psB[:, 512:1024]
                                i = vector.tensor_tensor(
                                    x_fm[dk][:, j * 128:(j + 1) * 128],
                                    x_fm[dk][:, j * 128:(j + 1) * 128],
                                    bank[:, (dk % 4) * 128:(dk % 4 + 1) * 128],
                                    ALU.add)
                            inc(i, "vp", 1, e=f"{tag}_resadd_{gi}_{qbi}")
                    mark(f"{tag}_done_v", "vp")

            if DO_FFN1:
                emit_ffn(0, pre=["setup_v"])
            else:
                st["ev"].setdefault("f0_done", ("vp", 0))
            if DO_ATTN:
                emit_qkv()
                emit_attn()
            if DO_FFN2:
                pre2 = (["at_done_t", "at_done_v"] if DO_ATTN else ["f0_done"])
                emit_ffn(1, pre=pre2)

            @block.sync
            def _(sync):
                if DO_FFN2:
                    wait(sync, "f1_done")
                elif DO_ATTN:
                    wait(sync, "at_done_v")
                    wait(sync, "at_done_t")
                elif DO_FFN1:
                    wait(sync, "f0_done")
                base = st["cnt"]["dl"]
                nst = NKD
                for k in range(NKD):
                    i = sync.dma_start(out=outT[k * 128:(k + 1) * 128, :],
                                       in_=x_fm[k])
                    inc(i, "dl", 16)
                if _DEBUG:
                    for src, dst, cols in [(samax, 0, 1), (amax_bc, 1, 1),
                                           (s_q, 2, 1), (inv_q, 3, 1),
                                           (s_q2, 4, 1), (stmp1, 5, 1)]:

                        i = sync.dma_start(out=dbg_scal[:, dst:dst + cols], in_=src, allow_slow_non_contiguous=True)
                        inc(i, "dl", 16)
                        nst += 1
                    for src, dst in [(invs_all, 0), (woamax, 32)]:
                        i = sync.dma_start(out=dbg_row[:, dst:dst + 32], in_=src)
                        inc(i, "dl", 16)
                        nst += 1
                    i = sync.dma_start(out=dbg_xn, in_=xn[0])
                    inc(i, "dl", 16)
                    nst += 1
                    i = sync.dma_start(out=dbg_hid, in_=hidb[0])
                    inc(i, "dl", 16)
                    nst += 1
                    i = sync.dma_start(out=dbg_sbc_d, in_=dbg_sbc_t)
                    inc(i, "dl", 16)
                    nst += 1
                    i = sync.dma_start(out=dbg_sr, in_=srow128)
                    inc(i, "dl", 16)
                    nst += 1
                    for mm_ in range(32):
                        i = sync.dma_start(out=dbg_hd[mm_], in_=hidden[0, mm_])
                        inc(i, "dl", 16)
                        nst += 1
                    i = sync.dma_start(out=dbg_ci, in_=cc_am_in[0].ap())
                    inc(i, "dl", 16)
                    nst += 1
                    i = sync.dma_start(out=dbg_co, in_=cc_am_out[0].ap())
                    inc(i, "dl", 16)
                    nst += 1
                if not st["dry"]:
                    sync.wait_ge(sems["dl"], base + nst * 16)

        st["dry"] = True
        st["cnt"] = {n: 0 for n in sems}
        emit_all(_FakeBlock())
        st["dry"] = False
        st["cnt"] = {n: 0 for n in sems}
        with nc.Block() as block:
            emit_all(block)

    return nc


_NC_CACHE = {}


def _get_nc():
    if "nc" not in _NC_CACHE:
        _NC_CACHE["nc"] = build_nc()
    return _NC_CACHE["nc"]


def kernel(L0_attn_x, L0_attn_W_qkv, L0_ffn_W_in, L0_ffn_b_in, L0_ffn_W_out,
           L0_ffn_b_out):
    x = np.ascontiguousarray(np.asarray(L0_attn_x, dtype=np.float32))
    W_qkv = np.asarray(L0_attn_W_qkv, dtype=np.float32)
    W_in = np.asarray(L0_ffn_W_in, dtype=np.float32)
    b_in = np.asarray(L0_ffn_b_in, dtype=np.float32)
    W_out = np.asarray(L0_ffn_W_out, dtype=np.float32)
    b_out = np.asarray(L0_ffn_b_out, dtype=np.float32)
    B, N, _ = x.shape

    w_in_rs = np.ascontiguousarray(W_in.reshape(D, NG, 512).transpose(1, 0, 2))
    w_qkv_rs = np.ascontiguousarray(
        np.concatenate([W_qkv[:, D:2 * D], W_qkv[:, 2 * D:3 * D], W_qkv[:, 0:D]],
                       axis=1).reshape(D, 6, 512).transpose(1, 0, 2))
    b_in_fm = np.ascontiguousarray(b_in.reshape(32, 128).T)
    b_out_fm = np.ascontiguousarray(b_out.reshape(8, 128).T)
    w_out_c = np.ascontiguousarray(W_out)
    ident32 = np.eye(128, dtype=np.float32)
    ident16 = np.eye(128, dtype=np.float16)
    col = np.arange(512)
    kr = 128 * (2 * ((col % 256) // 128) + col // 256) + col % 128

    in_maps = []
    for c in range(8):
        b, r = c // 2, c % 2
        rows = np.ascontiguousarray(x[b].reshape(32, 128, D)[r::2].reshape(T, D))
        p = np.arange(128)[:, None]
        mask = np.empty((128, 1024), dtype=np.float32)
        for jpar in range(2):
            vis = kr[None, :] <= p + 128 * r + 256 * jpar
            mask[:, jpar * 512:(jpar + 1) * 512] = np.where(vis, 0.0, -30000.0)
        in_maps.append({
            "xT": np.ascontiguousarray(rows.T),
            "w_in_r": w_in_rs,
            "w_out_d": w_out_c,
            "w_qkv_r": w_qkv_rs,
            "b_in_d": b_in_fm,
            "b_outh_d": b_out_fm,
            "roff_d": np.full((128, 1), 128.0 * r, dtype=np.float32),
            "ident16_d": ident16,
            "ident32_d": ident32,
            "maskc_d": mask,
        })

    nc = _get_nc()
    res = run_bass_kernel_spmd(nc, in_maps, core_ids=list(range(8)))

    out = np.empty((B, N, D), dtype=np.float32)
    for c in range(8):
        b, r = c // 2, c % 2
        rows = np.asarray(res.results[c]["outT"]).T
        out[b].reshape(32, 128, D)[r::2] = rows.reshape(NTB, 128, D)
    return out


# version 07
# speedup vs baseline: 1.0050x; 1.0050x over previous
import numpy as np
from contextlib import ExitStack

import concourse.bass as bass
import concourse.mybir as mybir
from concourse.bass_utils import run_bass_kernel_spmd

F32 = mybir.dt.float32
F16 = mybir.dt.float16
ALU = mybir.AluOpType
AF = mybir.ActivationFunctionType
AX = mybir.AxisListType

D = 1024
DFF = 4096
T = 2048
NKD = 8
NMF = 32
NTB = 16
NG = 8
C_MAGIC = 12582912.0
EPS = 1e-5
KB = 1024

import os
_STAGES = os.environ.get("MK_STAGES", "111")
_FFN_CUT = os.environ.get("MK_FFN_CUT", "full")
_DEBUG = os.environ.get("MK_DEBUG", "0") == "1"
DO_FFN1 = _STAGES[0] == "1"
DO_ATTN = _STAGES[1] == "1"
DO_FFN2 = _STAGES[2] == "1"

RG2 = [[0, 1], [2, 3], [4, 5], [6, 7]]
RG8 = [list(range(8))]


class _FakeIns:
    def then_inc(self, sem, n):
        return self


class _FakeEng:
    def __getattr__(self, name):
        if name in ("wait_ge",):
            return lambda *a, **k: None
        return lambda *a, **k: _FakeIns()


class _FakeBlock:
    def _run(self, f):
        f(_FakeEng())
    def sync(self, f): self._run(f)
    def vector(self, f): self._run(f)
    def scalar(self, f): self._run(f)
    def tensor(self, f): self._run(f)
    def gpsimd(self, f): self._run(f)


def build_nc():
    nc = bass.Bass("TRN2", target_bir_lowering=False, detect_race_conditions=False)

    xT = nc.dram_tensor("xT", [D, T], F32, kind="ExternalInput").ap()
    w_in_r = nc.dram_tensor("w_in_r", [NG, D, 512], F32, kind="ExternalInput").ap()
    w_out_d = nc.dram_tensor("w_out_d", [DFF, D], F32, kind="ExternalInput").ap()
    w_qkv_r = nc.dram_tensor("w_qkv_r", [6, D, 512], F32, kind="ExternalInput").ap()
    b_in_d = nc.dram_tensor("b_in_d", [128, 32], F32, kind="ExternalInput").ap()
    b_outh_d = nc.dram_tensor("b_outh_d", [128, 8], F32, kind="ExternalInput").ap()
    roff_d = nc.dram_tensor("roff_d", [128, 1], F32, kind="ExternalInput").ap()
    ident16_d = nc.dram_tensor("ident16_d", [128, 128], F16, kind="ExternalInput").ap()
    ident32_d = nc.dram_tensor("ident32_d", [128, 128], F32, kind="ExternalInput").ap()
    maskc_d = nc.dram_tensor("maskc_d", [128, 1024], F32, kind="ExternalInput").ap()
    outT = nc.dram_tensor("outT", [D, T], F32, kind="ExternalOutput").ap()
    if _DEBUG:
        dbg_scal = nc.dram_tensor("dbg_scal", [128, 16], F32, kind="ExternalOutput").ap()
        dbg_row = nc.dram_tensor("dbg_row", [128, 64], F32, kind="ExternalOutput").ap()
        dbg_xn = nc.dram_tensor("dbg_xn", [128, T], F16, kind="ExternalOutput").ap()
        dbg_hid = nc.dram_tensor("dbg_hid", [128, 512], F16, kind="ExternalOutput").ap()
        dbg_sr = nc.dram_tensor("dbg_sr", [1, 128], F32, kind="ExternalOutput").ap()
        dbg_hd = nc.dram_tensor("dbg_hd", [32, 128, 512], F16, kind="ExternalOutput").ap()
        dbg_sbc_d = nc.dram_tensor("dbg_sbc_d", [128, 512], F32, kind="ExternalOutput").ap()
        dbg_ci = nc.dram_tensor("dbg_ci", [128], F32, kind="ExternalOutput").ap()
        dbg_co = nc.dram_tensor("dbg_co", [128], F32, kind="ExternalOutput").ap()

    hidden = nc.dram_tensor("hidden", [4, NMF, 128, 512], F16).ap()
    w_in_int_d = nc.dram_tensor("w_in_int_d", [NG, 128, 4096], F16).ap()
    w_out_int_d = nc.dram_tensor("w_out_int_d", [NMF, 128, D], F16).ap()
    cc_am_in = [nc.dram_tensor(f"cc_am_in{i}", [128], F32) for i in range(4)]
    cc_am_out = [nc.dram_tensor(f"cc_am_out{i}", [128], F32) for i in range(4)]
    cc_k_in = nc.dram_tensor("cc_k_in", [D, T], F16)
    cc_k_out = nc.dram_tensor("cc_k_out", [2 * D, T], F16)
    cc_v_in = nc.dram_tensor("cc_v_in", [T, D], F16)
    cc_vo = nc.dram_tensor("cc_vo", [2 * T, D], F16)

    SB_BASE = 16512

    def at(name, shape, dtype, off):
        assert SB_BASE + off + shape[1] * mybir.dt.size(dtype) <= 229344, name
        return nc.alloc_sbuf_tensor_at(name, shape, dtype, offset=SB_BASE + off).ap()

    x_fm = [at(f"x_fm{k}", [128, T], F32, k * 8 * KB) for k in range(NKD)]
    mean_b = at("mean_b", [128, T], F32, 64 * KB)
    rstd_b = at("rstd_b", [128, T], F32, 72 * KB)
    M0 = 80 * KB
    sb_bin = at("sb_bin", [128, 32], F32, M0 + 0)
    sb_bouth = at("sb_bouth", [128, 8], F32, M0 + 128)
    sb_roff = at("sb_roff", [128, 1], F32, M0 + 160)
    sb_const = at("sb_const", [128, 16], F32, M0 + 192)
    ones_f16 = at("ones_f16", [128, 128], F16, M0 + 256)
    ones_f32k1 = at("ones_f32k1", [1, 128], F32, M0 + 512)
    ones_f32_128 = at("ones_f32_128", [128, 128], F32, M0 + 1024)
    ident_f16 = at("ident_f16", [128, 128], F16, M0 + 1536)
    ident_f32 = at("ident_f32", [128, 128], F32, M0 + 1792)
    masks = at("masks", [128, 1024], F32, M0 + 3328)
    invs_all = at("invs_all", [128, 32], F32, M0 + 7424)
    woamax = at("woamax", [128, 32], F32, M0 + 7552)
    comb = at("comb", [128, 32], F32, M0 + 7680)
    samax = at("samax", [128, 1], F32, M0 + 7808)
    stmp1 = at("stmp1", [128, 1], F32, M0 + 7840)
    stmp2 = at("stmp2", [128, 1], F32, M0 + 7872)
    amax_bc = at("amax_bc", [128, 1], F32, M0 + 7904)
    s_q = at("s_q", [128, 1], F32, M0 + 7936)
    inv_q = at("inv_q", [128, 1], F32, M0 + 7968)
    s_q2 = at("s_q2", [128, 1], F32, M0 + 8000)
    sone = at("sone", [1, 1], F32, M0 + 8032)
    srow128 = at("srow128", [1, 128], F32, M0 + 8064)
    rl = at("rl", [128, 16], F32, M0 + 8576)
    wamax4 = at("wamax4", [128, 4], F32, M0 + 8640)
    sfm4 = at("sfm4", [128, 4], F32, M0 + 8672)
    diag = at("diag", [128, 512], F32, M0 + 8704)
    tmpq = at("tmpq", [128, 1024], F32, M0 + 10752)
    hsc = at("hsc", [128, 128], F32, M0 + 14848)
    dbg_sbc_t = at("dbg_sbc_t", [128, 512], F32, M0 + 15360)
    vdrain = at("vdrain", [128, 256], F32, M0 + 17408)
    AB = 98 * KB
    xn = [at(f"xn{k}", [128, T], F16, AB + k * 4 * KB) for k in range(NKD)]
    wg = [at(f"wg{i}", [128, 4096], F32, AB + 32 * KB + i * 16 * KB) for i in range(2)]
    w_int_g = [at(f"w_int_g{i}", [128, 4096], F16, AB + 64 * KB + i * 8 * KB)
               for i in range(2)]
    hidb = [at(f"hidb{i}", [128, 512], F16, AB + 80 * KB + i * KB) for i in range(4)]
    wog = at("wog", [128, 4096], F32, AB + 84 * KB)
    lnxb = [at(f"lnxb{i}", [128, 1024], F16, AB + 100 * KB + i * 2 * KB)
            for i in range(2)]
    lnsq = [at(f"lnsq{i}", [128, 1024], F16, AB + 104 * KB + i * 2 * KB)
            for i in range(2)]
    harena = at("harena", [128, NMF * 512], F16, AB)
    wo_res = [at(f"wo_res{k}", [128, D], F16, AB + 32 * KB + k * 2 * KB)
              for k in range(NMF)]
    res_t = [at(f"res_t{i}", [128, 512], F32, AB + 96 * KB + i * 2 * KB)
             for i in range(2)]
    q_fm = [at(f"q_fm{k}", [128, T], F16, AB + 32 * KB + k * 4 * KB)
            for k in range(NKD)]
    qkv_ev = [at(f"qkv_ev{i}", [128, 512], F16, AB + 64 * KB + i * KB)
              for i in range(4)]
    wqkv_g = at("wqkv_g", [128, 4096], F16, AB + 68 * KB)
    srows = [at(f"srows{i}", [128, 4096], F16, AB + i * 8 * KB) for i in range(2)]
    prows = [at(f"prows{i}", [128, 4096], F16, AB + 16 * KB + i * 8 * KB)
             for i in range(2)]
    ktile = [at(f"ktile{i}", [128, 4096], F16, AB + 64 * KB + i * 8 * KB)
             for i in range(2)]
    vtile = [at(f"vtile{i}", [128, D], F16, AB + 80 * KB + i * 2 * KB)
             for i in range(2)]
    pt_sb = [at(f"pt_sb{i}", [128, 1024], F16, AB + 84 * KB + i * 2 * KB)
             for i in range(2)]
    av_t = [at(f"av_t{i}", [128, D], F32, AB + 88 * KB + i * 4 * KB)
            for i in range(2)]

    with ExitStack() as stack:
        E = stack.enter_context
        psA = E(nc.psum_tensor("psA", [128, 2048], F32))
        psB = E(nc.psum_tensor("psB", [128, 1024], F32))
        psPT = [E(nc.psum_tensor(f"psPT{i}", [128, 1024], F16)) for i in range(2)]
        slots = [psA[:, i * 512:(i + 1) * 512] for i in range(4)]

        sems = {n: E(nc.semaphore(n)) for n in ["dl", "dg", "cc", "tp", "vp", "sp", "gp"]}

        st = {"dry": True, "cnt": None, "ev": {}, "evfinal": None}

        def inc(ins, s, n=1, e=None):
            st["cnt"][s] += n
            if not st["dry"]:
                ins.then_inc(sems[s], n)
            if e is not None:
                if st["dry"]:
                    st["ev"][e] = (s, st["cnt"][s])
                else:
                    assert st["ev"][e] == (s, st["cnt"][s]), \
                        f"nondeterministic emission at event {e}"

        def mark(e, s):
            if st["dry"]:
                st["ev"][e] = (s, st["cnt"][s])
            else:
                assert st["ev"][e] == (s, st["cnt"][s]), f"mark drift at {e}"

        def wait(eng, e):
            if st["dry"]:
                return
            s, v = st["ev"][e]
            eng.wait_ge(sems[s], v)

        def vpad(vector):
            return vector.tensor_copy(vdrain[:, 4:5], ident_f32[:, 0:1])

        def emit_all(block):
            @block.sync
            def _(sync):
                for ap_, src in [(sb_bin, b_in_d), (sb_bouth, b_outh_d),
                                 (sb_roff, roff_d), (ident_f16, ident16_d),
                                 (ident_f32, ident32_d), (masks, maskc_d)]:
                    inc(sync.dma_start(out=ap_, in_=src), "dl", 16)
                for k in range(NKD):
                    inc(sync.dma_start(out=x_fm[k],
                                       in_=xT[k * 128:(k + 1) * 128, :]), "dl", 16)
                mark("x_loaded", "dl")

            @block.vector
            def _(vector):
                for col, val in [(0, EPS), (1, -0.5), (2, 1.0 / 1024.0),
                                 (3, 127.0), (4, 1.0 / 127.0), (5, 32767.0),
                                 (6, 1.0 / 32767.0), (7, 0.5), (8, 1.0 / 32.0)]:
                    vector.memset(sb_const[:, col:col + 1], val)
                vector.memset(ones_f16, 1.0)
                vector.memset(ones_f32k1, 1.0)
                vector.memset(ones_f32_128, 1.0)
                wait(vector, "x_loaded")
                i = vector.tensor_scalar_mul(sb_bouth, sb_bouth, sb_const[:, 7:8])
                inc(i, "vp", 1, e="setup_v")

            def emit_ln(tag, quant, pre):
                @block.vector
                def _(vector):
                    for e in pre:
                        wait(vector, e)
                    wait(vector, "setup_v")
                    for h in range(2):
                        for k in range(NKD):
                            kk = h * NKD + k
                            if kk >= 2:
                                wait(vector, f"{tag}_mm_{kk - 2}")
                            sl = slice(h * 1024, (h + 1) * 1024)
                            i = vector.tensor_copy(lnxb[k % 2][:], x_fm[k][:, sl])
                            inc(i, "vp", 1, e=f"{tag}_xb_{kk}")
                        wait(vector, f"{tag}_stats_{h}")
                        for sidx in range(2):
                            gsl = slice(h * 1024 + sidx * 512,
                                        h * 1024 + (sidx + 1) * 512)
                            vector.tensor_scalar_mul(mean_b[:, gsl], slots[sidx],
                                                     sb_const[:, 2:3])
                            vector.tensor_tensor(rstd_b[:, gsl], mean_b[:, gsl],
                                                 mean_b[:, gsl], ALU.mult)
                            i = vector.scalar_tensor_tensor(
                                rstd_b[:, gsl], slots[2 + sidx], sb_const[:, 2:3],
                                rstd_b[:, gsl], ALU.mult, ALU.subtract)
                            inc(i, "vp", 1, e=f"{tag}_var_{h}_{sidx}")

                @block.scalar
                def _(scalar):
                    for e in pre:
                        wait(scalar, e)
                    for h in range(2):
                        for k in range(NKD):
                            kk = h * NKD + k
                            if kk >= 2:
                                wait(scalar, f"{tag}_mm_{kk - 2}")
                            sl = slice(h * 1024, (h + 1) * 1024)
                            i = scalar.activation(lnsq[k % 2][:], x_fm[k][:, sl],
                                                  AF.Square)
                            inc(i, "sp", 1, e=f"{tag}_sq_{kk}")

                @block.tensor
                def _(tensor):
                    for e in pre:
                        wait(tensor, e)
                    for h in range(2):
                        if h > 0:
                            wait(tensor, f"{tag}_var_0_1")
                        for k in range(NKD):
                            kk = h * NKD + k
                            wait(tensor, f"{tag}_xb_{kk}")
                            wait(tensor, f"{tag}_sq_{kk}")
                            for s in range(2):
                                tsl = slice(s * 512, (s + 1) * 512)
                                tensor.matmul(slots[s], ones_f16,
                                              lnxb[k % 2][:, tsl],
                                              start=(k == 0), stop=(k == NKD - 1))
                                i = tensor.matmul(slots[2 + s], ones_f16,
                                                  lnsq[k % 2][:, tsl],
                                                  start=(k == 0), stop=(k == NKD - 1))
                            inc(i, "tp", 1, e=f"{tag}_mm_{kk}")
                        mark(f"{tag}_stats_{h}", "tp")

                @block.scalar
                def _(scalar):
                    for h in range(2):
                        wait(scalar, f"{tag}_var_{h}_1")
                        gsl = slice(h * 1024, (h + 1) * 1024)
                        scalar.activation(rstd_b[:, gsl], rstd_b[:, gsl], AF.Ln,
                                          bias=sb_const[:, 0:1], scale=1.0)
                        i = scalar.activation(rstd_b[:, gsl], rstd_b[:, gsl],
                                              AF.Exp, bias=0.0,
                                              scale=sb_const[:, 1:2])
                        inc(i, "sp", 1, e=f"{tag}_rstd_{h}")

                @block.vector
                def _(vector):
                    wait(vector, f"{tag}_rstd_0")
                    wait(vector, f"{tag}_rstd_1")
                    for k in range(NKD):
                        vector.tensor_tensor(xn[k], x_fm[k], mean_b, ALU.subtract)
                        i = vector.tensor_tensor(xn[k], xn[k], rstd_b, ALU.mult)
                        inc(i, "vp", 1, e=f"{tag}_xn_{k}")
                        if quant:
                            if k == 0:
                                i = vector.tensor_reduce(
                                    samax, xn[k], AX.X, ALU.max,
                                    apply_absolute_value=True)
                            else:
                                vector.tensor_reduce(stmp1, xn[k], AX.X, ALU.max,
                                                     apply_absolute_value=True)
                                i = vector.tensor_max(samax, samax, stmp1)
                    if quant:
                        inc(i, "vp", 1, e=f"{tag}_amax_local")
                    mark(f"{tag}_xn_all", "vp")

            ar_no = [0]

            def emit_amax_ar(tag, bc_ps):
                idx = ar_no[0]
                ar_no[0] += 1

                @block.gpsimd
                def _(gpsimd):
                    wait(gpsimd, f"{tag}_amax_local")
                    i = gpsimd.dma_start(out=cc_am_in[idx].ap(), in_=samax)
                    inc(i, "dg", 16, e=f"{tag}_arin")
                    wait(gpsimd, f"{tag}_arin")
                    i = gpsimd.collective_compute(
                        "AllReduce", ALU.max, replica_groups=RG8,
                        ins=[cc_am_in[idx].ap()], outs=[cc_am_out[idx].ap()])
                    inc(i, "cc", 1, e=f"{tag}_ar")
                    wait(gpsimd, f"{tag}_ar")
                    i = gpsimd.dma_start(out=srow128, in_=cc_am_out[idx].ap())
                    inc(i, "dg", 16, e=f"{tag}_arrow")

                @block.vector
                def _(vector):
                    wait(vector, f"{tag}_arrow")
                    vector.tensor_reduce(sone, srow128, AX.X, ALU.max)
                    i = vector.tensor_copy(vdrain[0:1, 0:1], sone)
                    inc(i, "vp", 1, e=f"{tag}_sone")

                @block.tensor
                def _(tensor):
                    wait(tensor, f"{tag}_sone")
                    i = tensor.matmul(bc_ps, ones_f32k1, sone, start=True,
                                      stop=True)
                    inc(i, "tp", 1, e=f"{tag}_bcmm")

                @block.scalar
                def _(scalar):
                    wait(scalar, f"{tag}_bcmm")
                    i = scalar.copy(amax_bc, bc_ps)
                    inc(i, "sp", 1, e=f"{tag}_amax_bc")

            def emit_qscales(tag, s_t, with_comb):
                @block.vector
                def _(vector):
                    wait(vector, f"{tag}_amax_bc")
                    vector.reciprocal(stmp2, amax_bc)
                    vector.tensor_scalar_mul(inv_q, amax_bc, sb_const[:, 6:7])
                    vpad(vector)
                    vpad(vector)
                    vector.tensor_scalar_mul(s_t, stmp2, sb_const[:, 5:6])
                    vpad(vector)
                    i = vpad(vector)
                    inc(i, "vp", 1, e=f"{tag}_scales")
                    if with_comb:
                        vpad(vector)
                        vector.tensor_scalar_mul(stmp2, inv_q, sb_const[:, 4:5])
                        vpad(vector)
                        vpad(vector)
                        i = vector.tensor_scalar_mul(comb, woamax, stmp2)
                        inc(i, "vp", 1, e=f"{tag}_comb")

            def emit_ffn(ci, pre):
                tag = f"f{ci}"
                ltag = f"lnf{ci}"
                first = ci == 0
                emit_ln(ltag, quant=True, pre=pre)
                if _FFN_CUT == "ln":
                    st["ev"][f"{tag}_done"] = st["ev"][f"{ltag}_xn_all"]
                    return
                emit_amax_ar(ltag, psB[:, 512:513])
                emit_qscales(ltag, s_q, with_comb=False)
                if _FFN_CUT == "ar":
                    st["ev"][f"{tag}_done"] = st["ev"][f"{ltag}_scales"]
                    return

                @block.vector
                def _(vector):
                    wait(vector, f"{ltag}_scales")
                    for k in range(NKD):
                        vector.tensor_scalar(mean_b, xn[k], s_q, C_MAGIC,
                                             ALU.mult, ALU.add)
                        i = vector.tensor_scalar(xn[k], mean_b, -C_MAGIC, inv_q,
                                                 ALU.add, ALU.mult)
                        inc(i, "vp", 1, e=f"{tag}_xsim_{k}")
                    mark(f"{tag}_xsim_all", "vp")
                if _FFN_CUT == "xsim":
                    st["ev"][f"{tag}_done"] = st["ev"][f"{tag}_xsim_all"]
                    return

                @block.sync
                def _(sync):
                    for g in range(NG):
                        if first:
                            if g >= 2:
                                wait(sync, f"{tag}_wgdone_{g - 2}")
                            elif g == 0:
                                for e in pre:
                                    wait(sync, e)
                            i = sync.dma_start(
                                out=wg[g % 2].rearrange("p (k f) -> p k f", f=512),
                                in_=w_in_r[g].rearrange("(k p) f -> p k f", p=128))
                            inc(i, "dl", 16, e=f"{tag}_wgld_{g}")
                        else:
                            if g >= 2:
                                wait(sync, f"{tag}_mm1_g{g - 2}")
                            elif g == 0:
                                for e in pre:
                                    wait(sync, e)
                            i = sync.dma_start(out=w_int_g[g % 2],
                                               in_=w_in_int_d[g])
                            inc(i, "dl", 16, e=f"{tag}_wintq_{g}")
                        if first:
                            if g > 0:
                                wait(sync, f"{tag}_wodone_{g - 1}")
                            else:
                                wait(sync, f"{ltag}_xn_all")
                            i = sync.dma_start(
                                out=wog.rearrange("p (k f) -> p k f", f=D),
                                in_=w_out_d[g * 512:(g + 1) * 512, :].rearrange(
                                    "(k p) f -> p k f", p=128))
                            inc(i, "dl", 16, e=f"{tag}_wold_{g}")

                @block.gpsimd
                def _(gpsimd):
                    for gp in range(NG):
                        for ml in range(4):
                            m = gp * 4 + ml
                            for t in range(4):
                                idx = (gp * 4 + ml) * 4 + t
                                wait(gpsimd, f"{tag}_gelu_{idx}")
                                i = gpsimd.dma_start(out=hidden[t, m],
                                                     in_=hidb[idx % 4])
                                inc(i, "dg", 16, e=f"{tag}_hidw_{idx}")
                        if first:
                            wait(gpsimd, f"{tag}_wintq_{gp}")
                            wait(gpsimd, f"{tag}_mm1_g{gp}")
                            i = gpsimd.dma_start(out=w_in_int_d[gp],
                                                 in_=w_int_g[gp % 2])
                            inc(i, "dg", 16, e=f"{tag}_wintw_{gp}")
                            for kk in range(4):
                                m = gp * 4 + kk
                                wait(gpsimd, f"{tag}_woq_{m}")
                                i = gpsimd.dma_start(out=w_out_int_d[m],
                                                     in_=lnxb[kk % 2])
                                inc(i, "dg", 16, e=f"{tag}_wow_{m}")
                    mark(f"{tag}_hidw_all", "dg")

                @block.vector
                def _(vector):
                    if first:
                        wait(vector, f"{ltag}_xn_all")
                    for g in range(NG):
                        if first:
                            wait(vector, f"{tag}_wgld_{g}")
                            w = wg[g % 2]
                            cm = rstd_b[:, 0:512]
                            cmn = rstd_b[:, 1024:1536]
                            vector.tensor_tensor(cm, w[:, 0:512], w[:, 512:1024],
                                                 ALU.max)
                            vector.tensor_tensor(cmn, w[:, 0:512], w[:, 512:1024],
                                                 ALU.min)
                            for k in range(2, NKD):
                                vector.tensor_tensor(
                                    cm, cm, w[:, k * 512:(k + 1) * 512], ALU.max)
                                vector.tensor_tensor(
                                    cmn, cmn, w[:, k * 512:(k + 1) * 512], ALU.min)
                            vector.tensor_scalar_mul(cmn, cmn, -1.0)
                            vector.tensor_tensor(cm, cm, cmn, ALU.max)
                            i = vector.tensor_copy(vdrain[:, 0:128],
                                                   cm[:, 384:512])
                            inc(i, "vp", 1, e=f"{tag}_cm_{g}")
                            wait(vector, f"{tag}_cmT_{g}")
                            for c in range(4):
                                vector.tensor_reduce(
                                    wamax4[:, c:c + 1],
                                    psB[:, c * 128:(c + 1) * 128], AX.X, ALU.max)
                            vpad(vector)
                            vpad(vector)
                            vector.reciprocal(sfm4, wamax4)
                            i = vector.tensor_scalar_mul(
                                invs_all[:, g * 4:g * 4 + 4], wamax4,
                                sb_const[:, 4:5])
                            inc(i, "vp", 1, e=f"{tag}_invs_{g}")
                            vpad(vector)
                            vpad(vector)
                            vector.tensor_scalar_mul(sfm4, sfm4, sb_const[:, 3:4])
                            vpad(vector)
                            vpad(vector)
                            for c in range(4):
                                vector.tensor_scalar_mul(
                                    diag[:, c * 128:(c + 1) * 128], ident_f32,
                                    sfm4[:, c:c + 1])
                            i = vector.tensor_copy(vdrain[:, 0:128],
                                                   diag[:, 384:512])
                            inc(i, "vp", 1, e=f"{tag}_diag_{g}")
                            wait(vector, f"{tag}_sbc_{g}")
                            if _DEBUG and g == 1:
                                vector.tensor_copy(dbg_sbc_t[:, 0:128],
                                                   psB[:, 384:512])
                                vector.tensor_copy(dbg_sbc_t[:, 128:256],
                                                   diag[:, 384:512])
                                vector.tensor_copy(dbg_sbc_t[:, 256:260], sfm4)
                            if g >= 2:
                                wait(vector, f"{tag}_mm1_g{g - 2}")
                                wait(vector, f"{tag}_wintw_{g - 2}")
                            for k in range(NKD):
                                ksl = slice(k * 512, (k + 1) * 512)
                                vector.tensor_tensor(rstd_b[:, 512:1024], w[:, ksl],
                                                     psB[:, 0:512], ALU.mult)
                                i = vector.tensor_scalar(
                                    w_int_g[g % 2][:, ksl], rstd_b[:, 512:1024],
                                    C_MAGIC, -C_MAGIC, ALU.add, ALU.add)
                            inc(i, "vp", 1, e=f"{tag}_wintq_{g}")
                            mark(f"{tag}_wgdone_{g}", "vp")
                        for idx in range(g * 16, g * 16 + 16):
                            wait(vector, f"{tag}_gelu_{idx}")
                            i = vector.tensor_reduce(hsc[:, idx:idx + 1],
                                                     hidb[idx % 4], AX.X, ALU.max)
                            inc(i, "vp", 1, e=f"{tag}_hamax_{idx}")
                        if first:
                            og = g
                            wait(vector, f"{tag}_wold_{og}")
                            for kk in range(4):
                                m = og * 4 + kk
                                wsl = wog[:, kk * D:(kk + 1) * D]
                                vector.tensor_reduce(woamax[:, m:m + 1], wsl, AX.X,
                                                     ALU.max,
                                                     apply_absolute_value=True)
                            vpad(vector)
                            vpad(vector)
                            vector.reciprocal(sfm4,
                                              woamax[:, og * 4:og * 4 + 4])
                            vpad(vector)
                            vpad(vector)
                            vector.tensor_scalar_mul(sfm4, sfm4, sb_const[:, 3:4])
                            vpad(vector)
                            vpad(vector)
                            for kk in range(4):
                                m = og * 4 + kk
                                if m >= 2:
                                    wait(vector, f"{tag}_wow_{m - 2}")
                                wsl = wog[:, kk * D:(kk + 1) * D]
                                vector.tensor_scalar(tmpq, wsl, sfm4[:, kk:kk + 1],
                                                     C_MAGIC, ALU.mult, ALU.add)
                                i = vector.tensor_scalar(lnxb[kk % 2], tmpq,
                                                         -C_MAGIC, None, ALU.add)
                                inc(i, "vp", 1, e=f"{tag}_woq_{m}")
                            mark(f"{tag}_wodone_{og}", "vp")
                    i = vector.tensor_reduce(samax, hsc, AX.X, ALU.max)
                    inc(i, "vp", 1)
                    mark(f"{ltag}h_amax_local", "vp")

                @block.tensor
                def _(tensor):
                    wait(tensor, f"{tag}_xsim_all")
                    idx = 0
                    for g in range(NG):
                        if first:
                            wait(tensor, f"{tag}_cm_{g}")
                            if g == 0:
                                wait(tensor, f"{ltag}_var_1_1")
                            else:
                                wait(tensor, f"{tag}_wintq_{g - 1}")
                            for c in range(4):
                                tensor.matmul(psB[:, c * 128:(c + 1) * 128],
                                              rstd_b[:, c * 128:(c + 1) * 128],
                                              ident_f32, start=(c == 0),
                                              stop=(c == 3), is_transpose=True)
                            i = tensor.matmul(psB[:, 512:513], ones_f32k1, sone,
                                              start=True, stop=True)
                            inc(i, "tp", 1, e=f"{tag}_cmT_{g}")
                            wait(tensor, f"{tag}_diag_{g}")
                            wait(tensor, f"{tag}_invs_{g}")
                            tensor.matmul(psB[:, 0:512], ones_f32_128, diag,
                                          start=True, stop=True)
                            i = tensor.matmul(psB[:, 512:513], ones_f32k1, sone,
                                              start=True, stop=True)
                            inc(i, "tp", 1, e=f"{tag}_sbc_{g}")
                        wait(tensor, f"{tag}_wintq_{g}")
                        for ml in range(4):
                            for t in range(4):
                                slot = slots[idx % 4]
                                if idx >= 4:
                                    wait(tensor, f"{tag}_gelu_{idx - 4}")
                                for k in range(NKD):
                                    lhsT = w_int_g[g % 2][
                                        :, k * 512 + ml * 128:k * 512 + (ml + 1) * 128]
                                    i = tensor.matmul(
                                        slot, lhsT, xn[k][:, t * 512:(t + 1) * 512],
                                        start=(k == 0), stop=(k == NKD - 1))
                                inc(i, "tp", 1, e=f"{tag}_mm1_{idx}")
                                idx += 1
                        mark(f"{tag}_mm1_g{g}", "tp")
                    mark(f"{tag}_mm1_all", "tp")

                @block.scalar
                def _(scalar):
                    idx = 0
                    for g in range(NG):
                        for ml in range(4):
                            m = g * 4 + ml
                            for t in range(4):
                                wait(scalar, f"{tag}_mm1_{idx}")
                                if idx >= 4:
                                    wait(scalar, f"{tag}_hidw_{idx - 4}")
                                    wait(scalar, f"{tag}_hamax_{idx - 4}")
                                i = scalar.activation(
                                    hidb[idx % 4], slots[idx % 4], AF.Gelu,
                                    bias=sb_bin[:, m:m + 1],
                                    scale=invs_all[:, m:m + 1])
                                inc(i, "sp", 1, e=f"{tag}_gelu_{idx}")
                                idx += 1

                if _FFN_CUT == "mm1":
                    st["ev"][f"{tag}_done"] = st["ev"][f"{tag}_hidw_all"]
                    return
                htag = f"{ltag}h"
                emit_amax_ar(htag, psB[:, 513:514])
                emit_qscales(htag, s_q2, with_comb=True)
                if _FFN_CUT == "ar2":
                    st["ev"][f"{tag}_done"] = st["ev"][f"{htag}_comb"]
                    return

                @block.sync
                def _(sync):
                    wait(sync, f"{tag}_mm1_all")
                    wait(sync, f"{tag}_hidw_all")
                    wait(sync, f"{ltag}h_amax_local")
                    if first:
                        wait(sync, f"{tag}_wow_{NMF - 1}")
                    for m in range(NMF):
                        i = sync.dma_start(out=wo_res[m], in_=w_out_int_d[m])
                        inc(i, "dl", 16, e=f"{tag}_wo2_{m}")
                    mark(f"{tag}_wo2_all", "dl")
                    for tq in range(4):
                        if tq >= 1:
                            wait(sync, f"{tag}_hfree_{tq - 1}")
                        i = sync.dma_start(
                            out=harena.rearrange("p (k f) -> p k f", f=512),
                            in_=hidden[tq].rearrange("k p f -> p k f"))
                        inc(i, "dl", 16, e=f"{tag}_hrd_{tq}")

                @block.vector
                def _(vector):
                    it = 0
                    for tq in range(4):
                        wait(vector, f"{tag}_hrd_{tq}")
                        wait(vector, f"{htag}_comb")
                        for k in range(NMF):
                            hsl = harena[:, k * 512:(k + 1) * 512]
                            vector.tensor_scalar(tmpq[:, 0:512], hsl, s_q2,
                                                 C_MAGIC, ALU.mult, ALU.add)
                            vector.tensor_scalar(hsl, tmpq[:, 0:512], -C_MAGIC,
                                                 comb[:, k:k + 1], ALU.add,
                                                 ALU.mult)
                            i = vector.tensor_copy(vdrain[:, 0:64],
                                                   hsl[:, 448:512])
                            inc(i, "vp", 1, e=f"{tag}_hq_{tq}_{k}")
                        if _FFN_CUT in ("mm2a", "mm2b"):
                            mark(f"{tag}_hfree_{tq}", "vp")
                            continue
                        tsl = slice(tq * 512, (tq + 1) * 512)
                        for mset in range(2):
                            for m4 in range(4):
                                m = mset * 4 + m4
                                j = it * 4 + m4
                                wait(vector, f"{tag}_resa_{it}_{m4}")
                                i = vector.tensor_tensor(x_fm[m][:, tsl],
                                                         x_fm[m][:, tsl],
                                                         res_t[j % 2], ALU.add)
                                inc(i, "vp", 1, e=f"{tag}_res_{j}")
                            it += 1
                        mark(f"{tag}_hfree_{tq}", "vp")
                    mark(f"{tag}_done", "vp")

                if _FFN_CUT == "mm2a":
                    return

                @block.tensor
                def _(tensor):
                    wait(tensor, f"{tag}_wo2_all")
                    it = 0
                    for tq in range(4):
                        for mset in range(2):
                            for k in range(NMF):
                                wait(tensor, f"{tag}_hq_{tq}_{k}")
                                for m4 in range(4):
                                    m = mset * 4 + m4
                                    if k == 0 and it >= 1 and _FFN_CUT == "full":
                                        wait(tensor, f"{tag}_resa_{it - 1}_{m4}")
                                    i = tensor.matmul(
                                        slots[m4],
                                        wo_res[k][:, m * 128:(m + 1) * 128],
                                        harena[:, k * 512:(k + 1) * 512],
                                        start=(k == 0), stop=(k == NMF - 1))
                                    if k == NMF - 1:
                                        inc(i, "tp", 1, e=f"{tag}_mm2_{it}_{m4}")
                            it += 1
                    mark(f"{tag}_mm2_all", "tp")

                if _FFN_CUT == "mm2b":
                    st["ev"][f"{tag}_done"] = st["ev"][f"{tag}_mm2_all"]
                    return

                @block.scalar
                def _(scalar):
                    it = 0
                    for tq in range(4):
                        for mset in range(2):
                            for m4 in range(4):
                                m = mset * 4 + m4
                                j = it * 4 + m4
                                wait(scalar, f"{tag}_mm2_{it}_{m4}")
                                if j >= 2:
                                    wait(scalar, f"{tag}_res_{j - 2}")
                                i = scalar.activation(res_t[j % 2], slots[m4],
                                                      AF.Identity,
                                                      bias=sb_bouth[:, m:m + 1],
                                                      scale=sb_const[:, 7:8])
                                inc(i, "sp", 1, e=f"{tag}_resa_{it}_{m4}")
                            it += 1

            def emit_qkv():
                tag = "qkv"
                emit_ln("lnq", quant=False, pre=["f0_done"])

                @block.gpsimd
                def _(gpsimd):
                    for g in range(6):
                        if g == 0:
                            wait(gpsimd, "lnq_xn_all")
                        else:
                            wait(gpsimd, f"{tag}_mm_g{g - 1}")
                        i = gpsimd.dma_start(
                            out=wqkv_g.rearrange("p (k f) -> p k f", f=512),
                            in_=w_qkv_r[g].rearrange("(k p) f -> p k f", p=128))
                        inc(i, "dg", 16, e=f"{tag}_wld_{g}")

                @block.tensor
                def _(tensor):
                    idx = 0
                    for g in range(6):
                        wait(tensor, f"{tag}_wld_{g}")
                        if g < 2 or g >= 4:
                            for ml in range(4):
                                for t in range(4):
                                    slot = slots[idx % 4]
                                    if idx >= 4:
                                        wait(tensor, f"{tag}_ev_{idx - 4}")
                                    for k in range(NKD):
                                        lhsT = wqkv_g[:, k * 512 + ml * 128:
                                                      k * 512 + (ml + 1) * 128]
                                        i = tensor.matmul(
                                            slot, lhsT,
                                            xn[k][:, t * 512:(t + 1) * 512],
                                            start=(k == 0), stop=(k == NKD - 1))
                                    inc(i, "tp", 1, e=f"{tag}_p_{idx}")
                                    idx += 1
                        else:
                            for tb in range(NTB):
                                slot = slots[idx % 4]
                                if idx >= 4:
                                    wait(tensor, f"{tag}_ev_{idx - 4}")
                                for k in range(NKD):
                                    i = tensor.matmul(
                                        slot, xn[k][:, tb * 128:(tb + 1) * 128],
                                        wqkv_g[:, k * 512:(k + 1) * 512],
                                        start=(k == 0), stop=(k == NKD - 1))
                                inc(i, "tp", 1, e=f"{tag}_p_{idx}")
                                idx += 1
                        mark(f"{tag}_mm_g{g}", "tp")

                @block.scalar
                def _(scalar):
                    idx = 0
                    for g in range(6):
                        nit = 16
                        for n in range(nit):
                            wait(scalar, f"{tag}_p_{idx}")
                            if g < 4:
                                if idx >= 4:
                                    wait(scalar, f"{tag}_evd_{idx - 4}")
                                i = scalar.copy(qkv_ev[idx % 4], slots[idx % 4])
                            else:
                                ml, t = n // 4, n % 4
                                m = (g - 4) * 4 + ml
                                i = scalar.copy(q_fm[m][:, t * 512:(t + 1) * 512],
                                                slots[idx % 4])
                            inc(i, "sp", 1, e=f"{tag}_ev_{idx}")
                            idx += 1

                @block.sync
                def _(sync):
                    idx = 0
                    for g in range(6):
                        if g < 2:
                            for ml in range(4):
                                m = g * 4 + ml
                                for t in range(4):
                                    wait(sync, f"{tag}_ev_{idx}")
                                    i = sync.dma_start(
                                        out=cc_k_in.ap()[m * 128:(m + 1) * 128,
                                                         t * 512:(t + 1) * 512],
                                        in_=qkv_ev[idx % 4])
                                    inc(i, "dl", 16, e=f"{tag}_evd_{idx}")
                                    idx += 1
                            if g == 1:
                                mark(f"{tag}_kin_done", "dl")
                        elif g < 4:
                            for tb in range(NTB):
                                wait(sync, f"{tag}_ev_{idx}")
                                i = sync.dma_start(
                                    out=cc_v_in.ap()[tb * 128:(tb + 1) * 128,
                                                     (g - 2) * 512:(g - 1) * 512],
                                    in_=qkv_ev[idx % 4])
                                inc(i, "dl", 16, e=f"{tag}_evd_{idx}")
                                idx += 1
                            if g == 3:
                                mark(f"{tag}_vin_done", "dl")
                        else:
                            idx += 16

                @block.gpsimd
                def _(gpsimd):
                    wait(gpsimd, f"{tag}_kin_done")
                    i = gpsimd.collective_compute(
                        "AllGather", ALU.bypass, replica_groups=RG2,
                        ins=[cc_k_in.ap()], outs=[cc_k_out.ap()])
                    inc(i, "cc", 1, e="ag_k")
                    wait(gpsimd, f"{tag}_vin_done")
                    i = gpsimd.collective_compute(
                        "AllGather", ALU.bypass, replica_groups=RG2,
                        ins=[cc_v_in.ap()], outs=[cc_vo.ap()])
                    inc(i, "cc", 1, e="ag_v")

            def emit_attn():
                tag = "at"

                @block.sync
                def _(sync):
                    kbuf = 0
                    vbuf = 0
                    for gi in range(8):
                        for c in range(gi + 1):
                            if kbuf < 2:
                                wait(sync, "ag_k")
                                wait(sync, "qkv_mm_g5")
                                wait(sync, "qkv_evd_63")
                            else:
                                wait(sync, f"{tag}_kused_{kbuf - 2}")
                            for h in range(2):
                                i = sync.dma_start(
                                    out=ktile[kbuf % 2].rearrange(
                                        "p (k f) -> p k f",
                                        f=512)[:, :, h * 256:(h + 1) * 256],
                                    in_=cc_k_out.ap()[h * 1024:(h + 1) * 1024,
                                                      2 * c * 128:(2 * c + 2) * 128]
                                    .rearrange("(k p) f -> p k f", p=128))
                                inc(i, "dl", 16)
                            mark(f"{tag}_kld_{kbuf}", "dl")
                            kbuf += 1
                        for blk in range(4 * (gi + 1)):
                            c, sub = blk // 4, blk % 4
                            h, jl = sub // 2, 2 * c + sub % 2
                            if vbuf < 2:
                                wait(sync, "ag_v")
                            else:
                                wait(sync, f"{tag}_vused_{vbuf - 2}")
                            i = sync.dma_start(
                                out=vtile[vbuf % 2],
                                in_=cc_vo.ap()[h * 2048 + jl * 128:
                                                  h * 2048 + (jl + 1) * 128, :])
                            inc(i, "dl", 16, e=f"{tag}_vld_{vbuf}")
                            vbuf += 1

                @block.tensor
                def _(tensor):
                    kbuf = 0
                    vbuf = 0
                    ptidx = 0
                    for gi in range(8):
                        for c in range(gi + 1):
                            wait(tensor, f"{tag}_kld_{kbuf}")
                            for qbi in range(2):
                                j = 2 * gi + qbi
                                sq = 2 * kbuf + qbi
                                if sq >= 4:
                                    wait(tensor, f"{tag}_sev_{sq - 4}")
                                for k in range(NKD):
                                    i = tensor.matmul(
                                        slots[sq % 4],
                                        q_fm[k][:, j * 128:(j + 1) * 128],
                                        ktile[kbuf % 2][:, k * 512:(k + 1) * 512],
                                        start=(k == 0), stop=(k == NKD - 1))
                                inc(i, "tp", 1, e=f"{tag}_qk_{sq}")
                            mark(f"{tag}_kused_{kbuf}", "tp")
                            kbuf += 1
                        nblk = 4 * (gi + 1)
                        for blk in range(nblk):
                            wait(tensor, f"{tag}_vld_{vbuf}")
                            if ptidx >= 2:
                                wait(tensor, f"{tag}_ptev_{ptidx - 2}")
                            for qbi in range(2):
                                wait(tensor, f"{tag}_p_{gi}_{qbi}")
                                i = tensor.matmul(
                                    psPT[blk % 2][:, qbi * 128:(qbi + 1) * 128],
                                    prows[qbi][:, blk * 128:(blk + 1) * 128],
                                    ident_f16, start=(qbi == 0), stop=(qbi == 1),
                                    is_transpose=True)
                            inc(i, "tp", 1, e=f"{tag}_pt_{ptidx}")
                            wait(tensor, f"{tag}_ptev_{ptidx}")
                            for qbi in range(2):
                                avps = psB[:, 0:1024] if qbi == 0 else psA[:, 1024:2048]
                                for half in range(2):
                                    i = tensor.matmul(
                                        avps[:, half * 512:(half + 1) * 512],
                                        pt_sb[blk % 2][:,
                                                       qbi * 128:(qbi + 1) * 128],
                                        vtile[vbuf % 2][:,
                                                        half * 512:(half + 1) * 512],
                                        start=(blk == 0), stop=(blk == nblk - 1))
                            inc(i, "tp", 1)
                            mark(f"{tag}_vused_{vbuf}", "tp")
                            vbuf += 1
                            ptidx += 1
                        mark(f"{tag}_av_done_{gi}", "tp")
                        for qbi in range(2):
                            wait(tensor, f"{tag}_avev_{gi}_{qbi}")
                            for dk in range(NKD):
                                bank = psB[:, 0:512] if dk < 4 else psB[:, 512:1024]
                                i = tensor.matmul(
                                    bank[:, (dk % 4) * 128:(dk % 4 + 1) * 128],
                                    av_t[qbi][:, dk * 128:(dk + 1) * 128],
                                    ident_f32, start=(dk % 4 == 0),
                                    stop=(dk % 4 == 3), is_transpose=True)
                            inc(i, "tp", 1, e=f"{tag}_avT_{gi}_{qbi}")
                            wait(tensor, f"{tag}_resadd_{gi}_{qbi}")
                    mark(f"{tag}_done_t", "tp")

                @block.scalar
                def _(scalar):
                    kbuf = 0
                    for gi in range(8):
                        for c in range(gi + 1):
                            for qbi in range(2):
                                sq = 2 * kbuf + qbi
                                wait(scalar, f"{tag}_qk_{sq}")
                                if sq == 0:
                                    wait(scalar, "qkv_mm_g5")
                                i = scalar.activation(
                                    srows[qbi][:, c * 512:(c + 1) * 512],
                                    slots[sq % 4], AF.Identity, bias=0.0,
                                    scale=sb_const[:, 8:9])
                                inc(i, "sp", 1, e=f"{tag}_sev_{sq}")
                            kbuf += 1
                        for qbi in range(2):
                            wait(scalar, f"{tag}_negmax_{gi}_{qbi}")
                            i = scalar.activation(
                                prows[qbi][:, 0:(gi + 1) * 512],
                                srows[qbi][:, 0:(gi + 1) * 512], AF.Exp,
                                bias=(stmp1 if qbi == 0 else stmp2), scale=1.0)
                            inc(i, "sp", 1, e=f"{tag}_exp_{gi}_{qbi}")
                        for qbi in range(2):
                            j = 2 * gi + qbi
                            wait(scalar, f"{tag}_av_done_{gi}")
                            wait(scalar, f"{tag}_rl_{gi}_{qbi}")
                            avps = psB[:, 0:1024] if qbi == 0 else psA[:, 1024:2048]
                            i = scalar.activation(av_t[qbi], avps, AF.Identity,
                                                  bias=0.0, scale=rl[:, j:j + 1])
                            inc(i, "sp", 1, e=f"{tag}_avev_{gi}_{qbi}")

                @block.vector
                def _(vector):
                    ptidx = 0
                    for gi in range(8):
                        last_kbuf = (gi + 1) * (gi + 2) // 2 - 1
                        for qbi in range(2):
                            j = 2 * gi + qbi
                            wait(vector, f"{tag}_sev_{2 * last_kbuf + qbi}")
                            vector.tensor_tensor(
                                srows[qbi][:, gi * 512:(gi + 1) * 512],
                                srows[qbi][:, gi * 512:(gi + 1) * 512],
                                masks[:, qbi * 512:(qbi + 1) * 512], ALU.add)
                            vector.tensor_reduce(
                                (stmp1 if qbi == 0 else stmp2),
                                srows[qbi][:, 0:(gi + 1) * 512], AX.X, ALU.max,
                                negate=True)
                            i = vector.tensor_copy(
                                vdrain[:, 0:1], stmp1 if qbi == 0 else stmp2)
                            inc(i, "vp", 1, e=f"{tag}_negmax_{gi}_{qbi}")
                            wait(vector, f"{tag}_exp_{gi}_{qbi}")
                            vector.tensor_reduce(samax,
                                                 prows[qbi][:, 0:(gi + 1) * 512],
                                                 AX.X, ALU.add)
                            vpad(vector)
                            vpad(vector)
                            vector.reciprocal(rl[:, j:j + 1], samax)
                            vpad(vector)
                            i = vpad(vector)
                            inc(i, "vp", 1, e=f"{tag}_rl_{gi}_{qbi}")
                            mark(f"{tag}_p_{gi}_{qbi}", "vp")
                        for blk in range(4 * (gi + 1)):
                            wait(vector, f"{tag}_pt_{ptidx}")
                            vector.tensor_copy(pt_sb[blk % 2][:, 0:256],
                                               psPT[blk % 2][:, 0:256])
                            i = vector.tensor_copy(
                                vdrain[:, 0:64], pt_sb[blk % 2][:, 192:256])
                            inc(i, "vp", 1, e=f"{tag}_ptev_{ptidx}")
                            ptidx += 1
                        for qbi in range(2):
                            j = 2 * gi + qbi
                            wait(vector, f"{tag}_avT_{gi}_{qbi}")
                            for dk in range(NKD):
                                bank = psB[:, 0:512] if dk < 4 else psB[:, 512:1024]
                                i = vector.tensor_tensor(
                                    x_fm[dk][:, j * 128:(j + 1) * 128],
                                    x_fm[dk][:, j * 128:(j + 1) * 128],
                                    bank[:, (dk % 4) * 128:(dk % 4 + 1) * 128],
                                    ALU.add)
                            inc(i, "vp", 1, e=f"{tag}_resadd_{gi}_{qbi}")
                    mark(f"{tag}_done_v", "vp")

            if DO_FFN1:
                emit_ffn(0, pre=["setup_v"])
            else:
                st["ev"].setdefault("f0_done", ("vp", 0))
            if DO_ATTN:
                emit_qkv()
                emit_attn()
            if DO_FFN2:
                pre2 = (["at_done_t", "at_done_v"] if DO_ATTN else ["f0_done"])
                emit_ffn(1, pre=pre2)

            @block.sync
            def _(sync):
                if DO_FFN2:
                    wait(sync, "f1_done")
                elif DO_ATTN:
                    wait(sync, "at_done_v")
                    wait(sync, "at_done_t")
                elif DO_FFN1:
                    wait(sync, "f0_done")
                base = st["cnt"]["dl"]
                nst = NKD
                for k in range(NKD):
                    i = sync.dma_start(out=outT[k * 128:(k + 1) * 128, :],
                                       in_=x_fm[k])
                    inc(i, "dl", 16)
                if _DEBUG:
                    for src, dst, cols in [(samax, 0, 1), (amax_bc, 1, 1),
                                           (s_q, 2, 1), (inv_q, 3, 1),
                                           (s_q2, 4, 1), (stmp1, 5, 1)]:

                        i = sync.dma_start(out=dbg_scal[:, dst:dst + cols], in_=src, allow_slow_non_contiguous=True)
                        inc(i, "dl", 16)
                        nst += 1
                    for src, dst in [(invs_all, 0), (woamax, 32)]:
                        i = sync.dma_start(out=dbg_row[:, dst:dst + 32], in_=src)
                        inc(i, "dl", 16)
                        nst += 1
                    i = sync.dma_start(out=dbg_xn, in_=xn[0])
                    inc(i, "dl", 16)
                    nst += 1
                    i = sync.dma_start(out=dbg_hid, in_=hidb[0])
                    inc(i, "dl", 16)
                    nst += 1
                    i = sync.dma_start(out=dbg_sbc_d, in_=dbg_sbc_t)
                    inc(i, "dl", 16)
                    nst += 1
                    i = sync.dma_start(out=dbg_sr, in_=srow128)
                    inc(i, "dl", 16)
                    nst += 1
                    for mm_ in range(32):
                        i = sync.dma_start(out=dbg_hd[mm_], in_=hidden[0, mm_])
                        inc(i, "dl", 16)
                        nst += 1
                    i = sync.dma_start(out=dbg_ci, in_=cc_am_in[0].ap())
                    inc(i, "dl", 16)
                    nst += 1
                    i = sync.dma_start(out=dbg_co, in_=cc_am_out[0].ap())
                    inc(i, "dl", 16)
                    nst += 1
                if not st["dry"]:
                    sync.wait_ge(sems["dl"], base + nst * 16)

        st["dry"] = True
        st["cnt"] = {n: 0 for n in sems}
        emit_all(_FakeBlock())
        st["dry"] = False
        st["cnt"] = {n: 0 for n in sems}
        with nc.Block() as block:
            emit_all(block)

    return nc


_NC_CACHE = {}


def _get_nc():
    if "nc" not in _NC_CACHE:
        _NC_CACHE["nc"] = build_nc()
    return _NC_CACHE["nc"]


def kernel(L0_attn_x, L0_attn_W_qkv, L0_ffn_W_in, L0_ffn_b_in, L0_ffn_W_out,
           L0_ffn_b_out):
    x = np.ascontiguousarray(np.asarray(L0_attn_x, dtype=np.float32))
    W_qkv = np.asarray(L0_attn_W_qkv, dtype=np.float32)
    W_in = np.asarray(L0_ffn_W_in, dtype=np.float32)
    b_in = np.asarray(L0_ffn_b_in, dtype=np.float32)
    W_out = np.asarray(L0_ffn_W_out, dtype=np.float32)
    b_out = np.asarray(L0_ffn_b_out, dtype=np.float32)
    B, N, _ = x.shape

    w_in_rs = np.ascontiguousarray(W_in.reshape(D, NG, 512).transpose(1, 0, 2))
    w_qkv_rs = np.ascontiguousarray(
        np.concatenate([W_qkv[:, D:2 * D], W_qkv[:, 2 * D:3 * D], W_qkv[:, 0:D]],
                       axis=1).reshape(D, 6, 512).transpose(1, 0, 2))
    b_in_fm = np.ascontiguousarray(b_in.reshape(32, 128).T)
    b_out_fm = np.ascontiguousarray(b_out.reshape(8, 128).T)
    w_out_c = np.ascontiguousarray(W_out)
    ident32 = np.eye(128, dtype=np.float32)
    ident16 = np.eye(128, dtype=np.float16)
    col = np.arange(512)
    kr = 128 * (2 * ((col % 256) // 128) + col // 256) + col % 128

    in_maps = []
    for c in range(8):
        b, r = c // 2, c % 2
        rows = np.ascontiguousarray(x[b].reshape(32, 128, D)[r::2].reshape(T, D))
        p = np.arange(128)[:, None]
        mask = np.empty((128, 1024), dtype=np.float32)
        for jpar in range(2):
            vis = kr[None, :] <= p + 128 * r + 256 * jpar
            mask[:, jpar * 512:(jpar + 1) * 512] = np.where(vis, 0.0, -30000.0)
        in_maps.append({
            "xT": np.ascontiguousarray(rows.T),
            "w_in_r": w_in_rs,
            "w_out_d": w_out_c,
            "w_qkv_r": w_qkv_rs,
            "b_in_d": b_in_fm,
            "b_outh_d": b_out_fm,
            "roff_d": np.full((128, 1), 128.0 * r, dtype=np.float32),
            "ident16_d": ident16,
            "ident32_d": ident32,
            "maskc_d": mask,
        })

    nc = _get_nc()
    res = run_bass_kernel_spmd(nc, in_maps, core_ids=list(range(8)))

    out = np.empty((B, N, D), dtype=np.float32)
    for c in range(8):
        b, r = c // 2, c % 2
        rows = np.asarray(res.results[c]["outT"]).T
        out[b].reshape(32, 128, D)[r::2] = rows.reshape(NTB, 128, D)
    return out


# version 10
# speedup vs baseline: 1.0635x; 1.0582x over previous
import numpy as np
from contextlib import ExitStack

import concourse.bass as bass
import concourse.mybir as mybir
from concourse.bass_utils import run_bass_kernel_spmd

F32 = mybir.dt.float32
F16 = mybir.dt.float16
ALU = mybir.AluOpType
AF = mybir.ActivationFunctionType
AX = mybir.AxisListType

D = 1024
DFF = 4096
T = 2048
NKD = 8
NMF = 32
NTB = 16
NG = 8
C_MAGIC = 12582912.0
EPS = 1e-5
KB = 1024

import os
_STAGES = os.environ.get("MK_STAGES", "111")
_FFN_CUT = os.environ.get("MK_FFN_CUT", "full")
_DEBUG = os.environ.get("MK_DEBUG", "0") == "1"
DO_FFN1 = _STAGES[0] == "1"
DO_ATTN = _STAGES[1] == "1"
DO_FFN2 = _STAGES[2] == "1"

RG2 = [[0, 1], [2, 3], [4, 5], [6, 7]]
RG8 = [list(range(8))]


class _FakeIns:
    def then_inc(self, sem, n):
        return self


class _FakeEng:
    def __getattr__(self, name):
        if name in ("wait_ge",):
            return lambda *a, **k: None
        return lambda *a, **k: _FakeIns()


class _FakeBlock:
    def _run(self, f):
        f(_FakeEng())
    def sync(self, f): self._run(f)
    def vector(self, f): self._run(f)
    def scalar(self, f): self._run(f)
    def tensor(self, f): self._run(f)
    def gpsimd(self, f): self._run(f)


def build_nc():
    nc = bass.Bass("TRN2", target_bir_lowering=False, detect_race_conditions=False)

    xT = nc.dram_tensor("xT", [D, T], F32, kind="ExternalInput").ap()
    w_in_r = nc.dram_tensor("w_in_r", [NG, D, 512], F32, kind="ExternalInput").ap()
    w_out_d = nc.dram_tensor("w_out_d", [DFF, D], F32, kind="ExternalInput").ap()
    w_qkv_r = nc.dram_tensor("w_qkv_r", [6, D, 512], F32, kind="ExternalInput").ap()
    b_in_d = nc.dram_tensor("b_in_d", [128, 32], F32, kind="ExternalInput").ap()
    b_outh_d = nc.dram_tensor("b_outh_d", [128, 8], F32, kind="ExternalInput").ap()
    roff_d = nc.dram_tensor("roff_d", [128, 1], F32, kind="ExternalInput").ap()
    ident16_d = nc.dram_tensor("ident16_d", [128, 128], F16, kind="ExternalInput").ap()
    ident32_d = nc.dram_tensor("ident32_d", [128, 128], F32, kind="ExternalInput").ap()
    maskc_d = nc.dram_tensor("maskc_d", [128, 1024], F32, kind="ExternalInput").ap()
    outT = nc.dram_tensor("outT", [D, T], F32, kind="ExternalOutput").ap()
    if _DEBUG:
        dbg_scal = nc.dram_tensor("dbg_scal", [128, 16], F32, kind="ExternalOutput").ap()
        dbg_row = nc.dram_tensor("dbg_row", [128, 64], F32, kind="ExternalOutput").ap()
        dbg_xn = nc.dram_tensor("dbg_xn", [128, T], F16, kind="ExternalOutput").ap()
        dbg_hid = nc.dram_tensor("dbg_hid", [128, 512], F16, kind="ExternalOutput").ap()
        dbg_sr = nc.dram_tensor("dbg_sr", [1, 128], F32, kind="ExternalOutput").ap()
        dbg_hd = nc.dram_tensor("dbg_hd", [32, 128, 512], F16, kind="ExternalOutput").ap()
        dbg_sbc_d = nc.dram_tensor("dbg_sbc_d", [128, 512], F32, kind="ExternalOutput").ap()
        dbg_ci = nc.dram_tensor("dbg_ci", [128], F32, kind="ExternalOutput").ap()
        dbg_co = nc.dram_tensor("dbg_co", [128], F32, kind="ExternalOutput").ap()

    hidden = nc.dram_tensor("hidden", [4, NMF, 128, 512], F16).ap()
    w_in_int_d = nc.dram_tensor("w_in_int_d", [NG, 128, 4096], F16).ap()
    w_out_int_d = nc.dram_tensor("w_out_int_d", [NMF, 128, D], F16).ap()
    cc_am_in = [nc.dram_tensor(f"cc_am_in{i}", [128], F32) for i in range(4)]
    cc_am_out = [nc.dram_tensor(f"cc_am_out{i}", [128], F32) for i in range(4)]
    cc_k_in = nc.dram_tensor("cc_k_in", [D, T], F16)
    cc_k_out = nc.dram_tensor("cc_k_out", [2 * D, T], F16)
    cc_v_in = nc.dram_tensor("cc_v_in", [T, D], F16)
    cc_vo = nc.dram_tensor("cc_vo", [2 * T, D], F16)

    SB_BASE = 16512

    def at(name, shape, dtype, off):
        assert SB_BASE + off + shape[1] * mybir.dt.size(dtype) <= 229344, name
        return nc.alloc_sbuf_tensor_at(name, shape, dtype, offset=SB_BASE + off).ap()

    x_fm = [at(f"x_fm{k}", [128, T], F32, k * 8 * KB) for k in range(NKD)]
    mean_b = at("mean_b", [128, T], F32, 64 * KB)
    rstd_b = at("rstd_b", [128, T], F32, 72 * KB)
    M0 = 80 * KB
    sb_bin = at("sb_bin", [128, 32], F32, M0 + 0)
    sb_bouth = at("sb_bouth", [128, 8], F32, M0 + 128)
    sb_roff = at("sb_roff", [128, 1], F32, M0 + 160)
    sb_const = at("sb_const", [128, 16], F32, M0 + 192)
    ones_f16 = at("ones_f16", [128, 128], F16, M0 + 256)
    ones_f32k1 = at("ones_f32k1", [1, 128], F32, M0 + 512)
    ones_f32_128 = at("ones_f32_128", [128, 128], F32, M0 + 1024)
    ident_f16 = at("ident_f16", [128, 128], F16, M0 + 1536)
    ident_f32 = at("ident_f32", [128, 128], F32, M0 + 1792)
    masks = at("masks", [128, 1024], F32, M0 + 3328)
    invs_all = at("invs_all", [128, 32], F32, M0 + 7424)
    woamax = at("woamax", [128, 32], F32, M0 + 7552)
    comb = at("comb", [128, 32], F32, M0 + 7680)
    samax = at("samax", [128, 1], F32, M0 + 7808)
    stmp1 = at("stmp1", [128, 1], F32, M0 + 7840)
    stmp2 = at("stmp2", [128, 1], F32, M0 + 7872)
    amax_bc = at("amax_bc", [128, 1], F32, M0 + 7904)
    s_q = at("s_q", [128, 1], F32, M0 + 7936)
    inv_q = at("inv_q", [128, 1], F32, M0 + 7968)
    s_q2 = at("s_q2", [128, 1], F32, M0 + 8000)
    sone = at("sone", [1, 1], F32, M0 + 8032)
    srow128 = at("srow128", [1, 128], F32, M0 + 8064)
    rl = at("rl", [128, 16], F32, M0 + 8576)
    wamax4 = at("wamax4", [128, 4], F32, M0 + 8640)
    sfm4 = at("sfm4", [128, 4], F32, M0 + 8672)
    diag = at("diag", [128, 512], F32, M0 + 8704)
    tmpq = at("tmpq", [128, 1024], F32, M0 + 10752)
    hsc = at("hsc", [128, 128], F32, M0 + 14848)
    dbg_sbc_t = at("dbg_sbc_t", [128, 512], F32, M0 + 15360)
    vdrain = at("vdrain", [128, 256], F32, M0 + 17408)
    AB = 98 * KB
    xn = [at(f"xn{k}", [128, T], F16, AB + k * 4 * KB) for k in range(NKD)]
    wg = [at(f"wg{i}", [128, 4096], F32, AB + 32 * KB + i * 16 * KB) for i in range(2)]
    w_int_g = [at(f"w_int_g{i}", [128, 4096], F16, AB + 64 * KB + i * 8 * KB)
               for i in range(2)]
    hidb = [at(f"hidb{i}", [128, 512], F16, AB + 80 * KB + i * KB) for i in range(4)]
    wog = at("wog", [128, 4096], F32, AB + 84 * KB)
    lnxb = [at(f"lnxb{i}", [128, 1024], F16, AB + 100 * KB + i * 2 * KB)
            for i in range(2)]
    lnsq = [at(f"lnsq{i}", [128, 1024], F16, AB + 104 * KB + i * 2 * KB)
            for i in range(2)]
    harena = at("harena", [128, NMF * 512], F16, AB)
    wo_res = [at(f"wo_res{k}", [128, D], F16, AB + 32 * KB + k * 2 * KB)
              for k in range(NMF)]
    res_t = [at(f"res_t{i}", [128, 512], F32, AB + 96 * KB + i * 2 * KB)
             for i in range(2)]
    q_fm = [at(f"q_fm{k}", [128, T], F16, AB + 32 * KB + k * 4 * KB)
            for k in range(NKD)]
    qkv_ev = [at(f"qkv_ev{i}", [128, 512], F16, AB + 64 * KB + i * KB)
              for i in range(4)]
    wqkv_g = at("wqkv_g", [128, 4096], F16, AB + 68 * KB)
    srows = [at(f"srows{i}", [128, 4096], F16, AB + i * 8 * KB) for i in range(2)]
    prows = [at(f"prows{i}", [128, 4096], F16, AB + 16 * KB + i * 8 * KB)
             for i in range(2)]
    _koff = [64 * KB, 72 * KB, 100 * KB]
    ktile = [at(f"ktile{i}", [128, 4096], F16, AB + _koff[i]) for i in range(3)]
    _voff = [80 * KB, 82 * KB, 96 * KB, 98 * KB]
    vtile = [at(f"vtile{i}", [128, D], F16, AB + _voff[i]) for i in range(4)]
    pt_sb = [at(f"pt_sb{i}", [128, 1024], F16, AB + 84 * KB + i * 2 * KB)
             for i in range(2)]
    av_t = [at(f"av_t{i}", [128, D], F32, AB + 88 * KB + i * 4 * KB)
            for i in range(2)]

    with ExitStack() as stack:
        E = stack.enter_context
        psA = E(nc.psum_tensor("psA", [128, 2048], F32))
        psB = E(nc.psum_tensor("psB", [128, 1024], F32))
        psPT = [E(nc.psum_tensor(f"psPT{i}", [128, 1024], F16)) for i in range(2)]
        slots = [psA[:, i * 512:(i + 1) * 512] for i in range(4)]

        sems = {n: E(nc.semaphore(n)) for n in ["dl", "dg", "cc", "tp", "vp", "sp", "gp"]}

        st = {"dry": True, "cnt": None, "ev": {}, "evfinal": None}

        def inc(ins, s, n=1, e=None):
            st["cnt"][s] += n
            if not st["dry"]:
                ins.then_inc(sems[s], n)
            if e is not None:
                if st["dry"]:
                    st["ev"][e] = (s, st["cnt"][s])
                else:
                    assert st["ev"][e] == (s, st["cnt"][s]), \
                        f"nondeterministic emission at event {e}"

        def mark(e, s):
            if st["dry"]:
                st["ev"][e] = (s, st["cnt"][s])
            else:
                assert st["ev"][e] == (s, st["cnt"][s]), f"mark drift at {e}"

        def wait(eng, e):
            if st["dry"]:
                return
            s, v = st["ev"][e]
            eng.wait_ge(sems[s], v)

        def vpad(vector):
            return vector.tensor_copy(vdrain[:, 4:5], ident_f32[:, 0:1])

        def emit_all(block):
            @block.sync
            def _(sync):
                for ap_, src in [(sb_bin, b_in_d), (sb_bouth, b_outh_d),
                                 (sb_roff, roff_d), (ident_f16, ident16_d),
                                 (ident_f32, ident32_d), (masks, maskc_d)]:
                    inc(sync.dma_start(out=ap_, in_=src), "dl", 16)
                for k in range(NKD):
                    inc(sync.dma_start(out=x_fm[k],
                                       in_=xT[k * 128:(k + 1) * 128, :]), "dl", 16)
                mark("x_loaded", "dl")

            @block.vector
            def _(vector):
                for col, val in [(0, EPS), (1, -0.5), (2, 1.0 / 1024.0),
                                 (3, 127.0), (4, 1.0 / 127.0), (5, 32767.0),
                                 (6, 1.0 / 32767.0), (7, 0.5), (8, 1.0 / 32.0)]:
                    vector.memset(sb_const[:, col:col + 1], val)
                vector.memset(ones_f16, 1.0)
                vector.memset(ones_f32k1, 1.0)
                vector.memset(ones_f32_128, 1.0)
                wait(vector, "x_loaded")
                i = vector.tensor_scalar_mul(sb_bouth, sb_bouth, sb_const[:, 7:8])
                inc(i, "vp", 1, e="setup_v")

            def emit_ln(tag, quant, pre):
                @block.vector
                def _(vector):
                    for e in pre:
                        wait(vector, e)
                    wait(vector, "setup_v")
                    for h in range(2):
                        for k in range(NKD):
                            kk = h * NKD + k
                            if kk >= 2:
                                wait(vector, f"{tag}_mm_{kk - 2}")
                            sl = slice(h * 1024, (h + 1) * 1024)
                            i = vector.tensor_copy(lnxb[k % 2][:], x_fm[k][:, sl])
                            inc(i, "vp", 1, e=f"{tag}_xb_{kk}")
                        wait(vector, f"{tag}_stats_{h}")
                        for sidx in range(2):
                            gsl = slice(h * 1024 + sidx * 512,
                                        h * 1024 + (sidx + 1) * 512)
                            vector.tensor_scalar_mul(mean_b[:, gsl], slots[sidx],
                                                     sb_const[:, 2:3])
                            vector.tensor_tensor(rstd_b[:, gsl], mean_b[:, gsl],
                                                 mean_b[:, gsl], ALU.mult)
                            i = vector.scalar_tensor_tensor(
                                rstd_b[:, gsl], slots[2 + sidx], sb_const[:, 2:3],
                                rstd_b[:, gsl], ALU.mult, ALU.subtract)
                            inc(i, "vp", 1, e=f"{tag}_var_{h}_{sidx}")

                @block.scalar
                def _(scalar):
                    for e in pre:
                        wait(scalar, e)
                    for h in range(2):
                        for k in range(NKD):
                            kk = h * NKD + k
                            if kk >= 2:
                                wait(scalar, f"{tag}_mm_{kk - 2}")
                            sl = slice(h * 1024, (h + 1) * 1024)
                            i = scalar.activation(lnsq[k % 2][:], x_fm[k][:, sl],
                                                  AF.Square)
                            inc(i, "sp", 1, e=f"{tag}_sq_{kk}")

                @block.tensor
                def _(tensor):
                    for e in pre:
                        wait(tensor, e)
                    for h in range(2):
                        if h > 0:
                            wait(tensor, f"{tag}_var_0_1")
                        for k in range(NKD):
                            kk = h * NKD + k
                            wait(tensor, f"{tag}_xb_{kk}")
                            wait(tensor, f"{tag}_sq_{kk}")
                            for s in range(2):
                                tsl = slice(s * 512, (s + 1) * 512)
                                tensor.matmul(slots[s], ones_f16,
                                              lnxb[k % 2][:, tsl],
                                              start=(k == 0), stop=(k == NKD - 1))
                                i = tensor.matmul(slots[2 + s], ones_f16,
                                                  lnsq[k % 2][:, tsl],
                                                  start=(k == 0), stop=(k == NKD - 1))
                            inc(i, "tp", 1, e=f"{tag}_mm_{kk}")
                        mark(f"{tag}_stats_{h}", "tp")

                @block.scalar
                def _(scalar):
                    for h in range(2):
                        wait(scalar, f"{tag}_var_{h}_1")
                        gsl = slice(h * 1024, (h + 1) * 1024)
                        scalar.activation(rstd_b[:, gsl], rstd_b[:, gsl], AF.Ln,
                                          bias=sb_const[:, 0:1], scale=1.0)
                        i = scalar.activation(rstd_b[:, gsl], rstd_b[:, gsl],
                                              AF.Exp, bias=0.0,
                                              scale=sb_const[:, 1:2])
                        inc(i, "sp", 1, e=f"{tag}_rstd_{h}")

                @block.vector
                def _(vector):
                    wait(vector, f"{tag}_rstd_0")
                    wait(vector, f"{tag}_rstd_1")
                    for k in range(NKD):
                        vector.tensor_tensor(xn[k], x_fm[k], mean_b, ALU.subtract)
                        i = vector.tensor_tensor(xn[k], xn[k], rstd_b, ALU.mult)
                        inc(i, "vp", 1, e=f"{tag}_xn_{k}")
                        if quant:
                            if k == 0:
                                i = vector.tensor_reduce(
                                    samax, xn[k], AX.X, ALU.max,
                                    apply_absolute_value=True)
                            else:
                                vector.tensor_reduce(stmp1, xn[k], AX.X, ALU.max,
                                                     apply_absolute_value=True)
                                i = vector.tensor_max(samax, samax, stmp1)
                    if quant:
                        inc(i, "vp", 1, e=f"{tag}_amax_local")
                    mark(f"{tag}_xn_all", "vp")

            ar_no = [0]

            def emit_amax_ar(tag, bc_ps):
                idx = ar_no[0]
                ar_no[0] += 1

                @block.gpsimd
                def _(gpsimd):
                    wait(gpsimd, f"{tag}_amax_local")
                    i = gpsimd.dma_start(out=cc_am_in[idx].ap(), in_=samax)
                    inc(i, "dg", 16, e=f"{tag}_arin")
                    wait(gpsimd, f"{tag}_arin")
                    i = gpsimd.collective_compute(
                        "AllReduce", ALU.max, replica_groups=RG8,
                        ins=[cc_am_in[idx].ap()], outs=[cc_am_out[idx].ap()])
                    inc(i, "cc", 1, e=f"{tag}_ar")
                    wait(gpsimd, f"{tag}_ar")
                    i = gpsimd.dma_start(out=srow128, in_=cc_am_out[idx].ap())
                    inc(i, "dg", 16, e=f"{tag}_arrow")

                @block.vector
                def _(vector):
                    wait(vector, f"{tag}_arrow")
                    vector.tensor_reduce(sone, srow128, AX.X, ALU.max)
                    i = vector.tensor_copy(vdrain[0:1, 0:1], sone)
                    inc(i, "vp", 1, e=f"{tag}_sone")

                @block.tensor
                def _(tensor):
                    wait(tensor, f"{tag}_sone")
                    i = tensor.matmul(bc_ps, ones_f32k1, sone, start=True,
                                      stop=True)
                    inc(i, "tp", 1, e=f"{tag}_bcmm")

                @block.scalar
                def _(scalar):
                    wait(scalar, f"{tag}_bcmm")
                    i = scalar.copy(amax_bc, bc_ps)
                    inc(i, "sp", 1, e=f"{tag}_amax_bc")

            def emit_qscales(tag, s_t, with_comb):
                @block.vector
                def _(vector):
                    wait(vector, f"{tag}_amax_bc")
                    vector.reciprocal(stmp2, amax_bc)
                    vector.tensor_scalar_mul(inv_q, amax_bc, sb_const[:, 6:7])
                    vpad(vector)
                    vpad(vector)
                    vector.tensor_scalar_mul(s_t, stmp2, sb_const[:, 5:6])
                    vpad(vector)
                    i = vpad(vector)
                    inc(i, "vp", 1, e=f"{tag}_scales")
                    if with_comb:
                        vpad(vector)
                        vector.tensor_scalar_mul(stmp2, inv_q, sb_const[:, 4:5])
                        vpad(vector)
                        vpad(vector)
                        i = vector.tensor_scalar_mul(comb, woamax, stmp2)
                        inc(i, "vp", 1, e=f"{tag}_comb")

            def emit_ffn(ci, pre):
                tag = f"f{ci}"
                ltag = f"lnf{ci}"
                first = ci == 0
                emit_ln(ltag, quant=True, pre=pre)
                if _FFN_CUT == "ln":
                    st["ev"][f"{tag}_done"] = st["ev"][f"{ltag}_xn_all"]
                    return
                emit_amax_ar(ltag, psB[:, 512:513])
                emit_qscales(ltag, s_q, with_comb=False)
                if _FFN_CUT == "ar":
                    st["ev"][f"{tag}_done"] = st["ev"][f"{ltag}_scales"]
                    return

                @block.vector
                def _(vector):
                    wait(vector, f"{ltag}_scales")
                    for k in range(NKD):
                        vector.tensor_scalar(mean_b, xn[k], s_q, C_MAGIC,
                                             ALU.mult, ALU.add)
                        i = vector.tensor_scalar(xn[k], mean_b, -C_MAGIC, inv_q,
                                                 ALU.add, ALU.mult)
                        inc(i, "vp", 1, e=f"{tag}_xsim_{k}")
                    mark(f"{tag}_xsim_all", "vp")
                if _FFN_CUT == "xsim":
                    st["ev"][f"{tag}_done"] = st["ev"][f"{tag}_xsim_all"]
                    return

                @block.sync
                def _(sync):
                    for g in range(NG):
                        if first:
                            if g >= 2:
                                wait(sync, f"{tag}_wgdone_{g - 2}")
                            elif g == 0:
                                for e in pre:
                                    wait(sync, e)
                            i = sync.dma_start(
                                out=wg[g % 2].rearrange("p (k f) -> p k f", f=512),
                                in_=w_in_r[g].rearrange("(k p) f -> p k f", p=128))
                            inc(i, "dl", 16, e=f"{tag}_wgld_{g}")
                        else:
                            if g >= 2:
                                wait(sync, f"{tag}_mm1_g{g - 2}")
                            elif g == 0:
                                for e in pre:
                                    wait(sync, e)
                            i = sync.dma_start(out=w_int_g[g % 2],
                                               in_=w_in_int_d[g])
                            inc(i, "dl", 16, e=f"{tag}_wintq_{g}")
                        if first:
                            if g > 0:
                                wait(sync, f"{tag}_wodone_{g - 1}")
                            else:
                                wait(sync, f"{ltag}_xn_all")
                            i = sync.dma_start(
                                out=wog.rearrange("p (k f) -> p k f", f=D),
                                in_=w_out_d[g * 512:(g + 1) * 512, :].rearrange(
                                    "(k p) f -> p k f", p=128))
                            inc(i, "dl", 16, e=f"{tag}_wold_{g}")

                @block.gpsimd
                def _(gpsimd):
                    for gp in range(NG):
                        for ml in range(4):
                            m = gp * 4 + ml
                            for t in range(4):
                                idx = (gp * 4 + ml) * 4 + t
                                wait(gpsimd, f"{tag}_gelu_{idx}")
                                i = gpsimd.dma_start(out=hidden[t, m],
                                                     in_=hidb[idx % 4])
                                inc(i, "dg", 16, e=f"{tag}_hidw_{idx}")
                        if first:
                            wait(gpsimd, f"{tag}_wintq_{gp}")
                            wait(gpsimd, f"{tag}_mm1_g{gp}")
                            i = gpsimd.dma_start(out=w_in_int_d[gp],
                                                 in_=w_int_g[gp % 2])
                            inc(i, "dg", 16, e=f"{tag}_wintw_{gp}")
                            for kk in range(4):
                                m = gp * 4 + kk
                                wait(gpsimd, f"{tag}_woq_{m}")
                                i = gpsimd.dma_start(out=w_out_int_d[m],
                                                     in_=lnxb[kk % 2])
                                inc(i, "dg", 16, e=f"{tag}_wow_{m}")
                    mark(f"{tag}_hidw_all", "dg")

                @block.vector
                def _(vector):
                    if first:
                        wait(vector, f"{ltag}_xn_all")
                    for g in range(NG):
                        if first:
                            wait(vector, f"{tag}_wgld_{g}")
                            w = wg[g % 2]
                            cm = rstd_b[:, 0:512]
                            cmn = rstd_b[:, 1024:1536]
                            vector.tensor_tensor(cm, w[:, 0:512], w[:, 512:1024],
                                                 ALU.max)
                            vector.tensor_tensor(cmn, w[:, 0:512], w[:, 512:1024],
                                                 ALU.min)
                            for k in range(2, NKD):
                                vector.tensor_tensor(
                                    cm, cm, w[:, k * 512:(k + 1) * 512], ALU.max)
                                vector.tensor_tensor(
                                    cmn, cmn, w[:, k * 512:(k + 1) * 512], ALU.min)
                            vector.tensor_scalar_mul(cmn, cmn, -1.0)
                            vector.tensor_tensor(cm, cm, cmn, ALU.max)
                            i = vector.tensor_copy(vdrain[:, 0:128],
                                                   cm[:, 384:512])
                            inc(i, "vp", 1, e=f"{tag}_cm_{g}")
                            wait(vector, f"{tag}_cmT_{g}")
                            for c in range(4):
                                vector.tensor_reduce(
                                    wamax4[:, c:c + 1],
                                    psB[:, c * 128:(c + 1) * 128], AX.X, ALU.max)
                            vpad(vector)
                            vpad(vector)
                            vector.reciprocal(sfm4, wamax4)
                            i = vector.tensor_scalar_mul(
                                invs_all[:, g * 4:g * 4 + 4], wamax4,
                                sb_const[:, 4:5])
                            inc(i, "vp", 1, e=f"{tag}_invs_{g}")
                            vpad(vector)
                            vpad(vector)
                            vector.tensor_scalar_mul(sfm4, sfm4, sb_const[:, 3:4])
                            vpad(vector)
                            vpad(vector)
                            for c in range(4):
                                vector.tensor_scalar_mul(
                                    diag[:, c * 128:(c + 1) * 128], ident_f32,
                                    sfm4[:, c:c + 1])
                            i = vector.tensor_copy(vdrain[:, 0:128],
                                                   diag[:, 384:512])
                            inc(i, "vp", 1, e=f"{tag}_diag_{g}")
                            wait(vector, f"{tag}_sbc_{g}")
                            if _DEBUG and g == 1:
                                vector.tensor_copy(dbg_sbc_t[:, 0:128],
                                                   psB[:, 384:512])
                                vector.tensor_copy(dbg_sbc_t[:, 128:256],
                                                   diag[:, 384:512])
                                vector.tensor_copy(dbg_sbc_t[:, 256:260], sfm4)
                            if g >= 2:
                                wait(vector, f"{tag}_mm1_g{g - 2}")
                                wait(vector, f"{tag}_wintw_{g - 2}")
                            for k in range(NKD):
                                ksl = slice(k * 512, (k + 1) * 512)
                                vector.tensor_tensor(rstd_b[:, 512:1024], w[:, ksl],
                                                     psB[:, 0:512], ALU.mult)
                                i = vector.tensor_scalar(
                                    w_int_g[g % 2][:, ksl], rstd_b[:, 512:1024],
                                    C_MAGIC, -C_MAGIC, ALU.add, ALU.add)
                            inc(i, "vp", 1, e=f"{tag}_wintq_{g}")
                            mark(f"{tag}_wgdone_{g}", "vp")
                        for idx in range(g * 16, g * 16 + 16):
                            wait(vector, f"{tag}_gelu_{idx}")
                            i = vector.tensor_reduce(hsc[:, idx:idx + 1],
                                                     hidb[idx % 4], AX.X, ALU.max)
                            inc(i, "vp", 1, e=f"{tag}_hamax_{idx}")
                        if first:
                            og = g
                            wait(vector, f"{tag}_wold_{og}")
                            for kk in range(4):
                                m = og * 4 + kk
                                wsl = wog[:, kk * D:(kk + 1) * D]
                                vector.tensor_reduce(woamax[:, m:m + 1], wsl, AX.X,
                                                     ALU.max,
                                                     apply_absolute_value=True)
                            vpad(vector)
                            vpad(vector)
                            vector.reciprocal(sfm4,
                                              woamax[:, og * 4:og * 4 + 4])
                            vpad(vector)
                            vpad(vector)
                            vector.tensor_scalar_mul(sfm4, sfm4, sb_const[:, 3:4])
                            vpad(vector)
                            vpad(vector)
                            for kk in range(4):
                                m = og * 4 + kk
                                if m >= 2:
                                    wait(vector, f"{tag}_wow_{m - 2}")
                                wsl = wog[:, kk * D:(kk + 1) * D]
                                vector.tensor_scalar(tmpq, wsl, sfm4[:, kk:kk + 1],
                                                     C_MAGIC, ALU.mult, ALU.add)
                                i = vector.tensor_scalar(lnxb[kk % 2], tmpq,
                                                         -C_MAGIC, None, ALU.add)
                                inc(i, "vp", 1, e=f"{tag}_woq_{m}")
                            mark(f"{tag}_wodone_{og}", "vp")
                    i = vector.tensor_reduce(samax, hsc, AX.X, ALU.max)
                    inc(i, "vp", 1)
                    mark(f"{ltag}h_amax_local", "vp")

                @block.tensor
                def _(tensor):
                    wait(tensor, f"{tag}_xsim_all")
                    idx = 0
                    for g in range(NG):
                        if first:
                            wait(tensor, f"{tag}_cm_{g}")
                            if g == 0:
                                wait(tensor, f"{ltag}_var_1_1")
                            else:
                                wait(tensor, f"{tag}_wintq_{g - 1}")
                            for c in range(4):
                                tensor.matmul(psB[:, c * 128:(c + 1) * 128],
                                              rstd_b[:, c * 128:(c + 1) * 128],
                                              ident_f32, start=(c == 0),
                                              stop=(c == 3), is_transpose=True)
                            i = tensor.matmul(psB[:, 512:513], ones_f32k1, sone,
                                              start=True, stop=True)
                            inc(i, "tp", 1, e=f"{tag}_cmT_{g}")
                            wait(tensor, f"{tag}_diag_{g}")
                            wait(tensor, f"{tag}_invs_{g}")
                            tensor.matmul(psB[:, 0:512], ones_f32_128, diag,
                                          start=True, stop=True)
                            i = tensor.matmul(psB[:, 512:513], ones_f32k1, sone,
                                              start=True, stop=True)
                            inc(i, "tp", 1, e=f"{tag}_sbc_{g}")
                        wait(tensor, f"{tag}_wintq_{g}")
                        for ml in range(4):
                            for t in range(4):
                                slot = slots[idx % 4]
                                if idx >= 4:
                                    wait(tensor, f"{tag}_gelu_{idx - 4}")
                                for k in range(NKD):
                                    lhsT = w_int_g[g % 2][
                                        :, k * 512 + ml * 128:k * 512 + (ml + 1) * 128]
                                    i = tensor.matmul(
                                        slot, lhsT, xn[k][:, t * 512:(t + 1) * 512],
                                        start=(k == 0), stop=(k == NKD - 1))
                                inc(i, "tp", 1, e=f"{tag}_mm1_{idx}")
                                idx += 1
                        mark(f"{tag}_mm1_g{g}", "tp")
                    mark(f"{tag}_mm1_all", "tp")

                @block.scalar
                def _(scalar):
                    idx = 0
                    for g in range(NG):
                        for ml in range(4):
                            m = g * 4 + ml
                            for t in range(4):
                                wait(scalar, f"{tag}_mm1_{idx}")
                                if idx >= 4:
                                    wait(scalar, f"{tag}_hidw_{idx - 4}")
                                    wait(scalar, f"{tag}_hamax_{idx - 4}")
                                i = scalar.activation(
                                    hidb[idx % 4], slots[idx % 4], AF.Gelu,
                                    bias=sb_bin[:, m:m + 1],
                                    scale=invs_all[:, m:m + 1])
                                inc(i, "sp", 1, e=f"{tag}_gelu_{idx}")
                                idx += 1

                if _FFN_CUT == "mm1":
                    st["ev"][f"{tag}_done"] = st["ev"][f"{tag}_hidw_all"]
                    return
                htag = f"{ltag}h"
                emit_amax_ar(htag, psB[:, 513:514])
                emit_qscales(htag, s_q2, with_comb=True)
                if _FFN_CUT == "ar2":
                    st["ev"][f"{tag}_done"] = st["ev"][f"{htag}_comb"]
                    return

                @block.sync
                def _(sync):
                    wait(sync, f"{tag}_mm1_all")
                    wait(sync, f"{tag}_hidw_all")
                    wait(sync, f"{ltag}h_amax_local")
                    if first:
                        wait(sync, f"{tag}_wow_{NMF - 1}")
                    for m in range(NMF):
                        i = sync.dma_start(out=wo_res[m], in_=w_out_int_d[m])
                        inc(i, "dl", 16, e=f"{tag}_wo2_{m}")
                    mark(f"{tag}_wo2_all", "dl")
                    for tq in range(4):
                        if tq >= 1:
                            wait(sync, f"{tag}_hfree_{tq - 1}")
                        i = sync.dma_start(
                            out=harena.rearrange("p (k f) -> p k f", f=512),
                            in_=hidden[tq].rearrange("k p f -> p k f"))
                        inc(i, "dl", 16, e=f"{tag}_hrd_{tq}")

                @block.vector
                def _(vector):
                    it = 0
                    for tq in range(4):
                        wait(vector, f"{tag}_hrd_{tq}")
                        wait(vector, f"{htag}_comb")
                        for k in range(NMF):
                            hsl = harena[:, k * 512:(k + 1) * 512]
                            vector.tensor_scalar(tmpq[:, 0:512], hsl, s_q2,
                                                 C_MAGIC, ALU.mult, ALU.add)
                            vector.tensor_scalar(hsl, tmpq[:, 0:512], -C_MAGIC,
                                                 comb[:, k:k + 1], ALU.add,
                                                 ALU.mult)
                            i = vector.tensor_copy(vdrain[:, 0:64],
                                                   hsl[:, 448:512])
                            inc(i, "vp", 1, e=f"{tag}_hq_{tq}_{k}")
                        if _FFN_CUT in ("mm2a", "mm2b"):
                            mark(f"{tag}_hfree_{tq}", "vp")
                            continue
                        tsl = slice(tq * 512, (tq + 1) * 512)
                        for mset in range(2):
                            for m4 in range(4):
                                m = mset * 4 + m4
                                j = it * 4 + m4
                                wait(vector, f"{tag}_resa_{it}_{m4}")
                                i = vector.tensor_tensor(x_fm[m][:, tsl],
                                                         x_fm[m][:, tsl],
                                                         res_t[j % 2], ALU.add)
                                inc(i, "vp", 1, e=f"{tag}_res_{j}")
                            it += 1
                        mark(f"{tag}_hfree_{tq}", "vp")
                    mark(f"{tag}_done", "vp")

                if _FFN_CUT == "mm2a":
                    return

                @block.tensor
                def _(tensor):
                    wait(tensor, f"{tag}_wo2_all")
                    it = 0
                    for tq in range(4):
                        for mset in range(2):
                            for k in range(NMF):
                                wait(tensor, f"{tag}_hq_{tq}_{k}")
                                for m4 in range(4):
                                    m = mset * 4 + m4
                                    if k == 0 and it >= 1 and _FFN_CUT == "full":
                                        wait(tensor, f"{tag}_resa_{it - 1}_{m4}")
                                    i = tensor.matmul(
                                        slots[m4],
                                        wo_res[k][:, m * 128:(m + 1) * 128],
                                        harena[:, k * 512:(k + 1) * 512],
                                        start=(k == 0), stop=(k == NMF - 1))
                                    if k == NMF - 1:
                                        inc(i, "tp", 1, e=f"{tag}_mm2_{it}_{m4}")
                            it += 1
                    mark(f"{tag}_mm2_all", "tp")

                if _FFN_CUT == "mm2b":
                    st["ev"][f"{tag}_done"] = st["ev"][f"{tag}_mm2_all"]
                    return

                @block.scalar
                def _(scalar):
                    it = 0
                    for tq in range(4):
                        for mset in range(2):
                            for m4 in range(4):
                                m = mset * 4 + m4
                                j = it * 4 + m4
                                wait(scalar, f"{tag}_mm2_{it}_{m4}")
                                if j >= 2:
                                    wait(scalar, f"{tag}_res_{j - 2}")
                                i = scalar.activation(res_t[j % 2], slots[m4],
                                                      AF.Identity,
                                                      bias=sb_bouth[:, m:m + 1],
                                                      scale=sb_const[:, 7:8])
                                inc(i, "sp", 1, e=f"{tag}_resa_{it}_{m4}")
                            it += 1

            def emit_qkv():
                tag = "qkv"
                emit_ln("lnq", quant=False, pre=["f0_done"])

                @block.gpsimd
                def _(gpsimd):
                    for g in range(6):
                        if g == 0:
                            wait(gpsimd, "lnq_xn_all")
                        else:
                            wait(gpsimd, f"{tag}_mm_g{g - 1}")
                        i = gpsimd.dma_start(
                            out=wqkv_g.rearrange("p (k f) -> p k f", f=512),
                            in_=w_qkv_r[g].rearrange("(k p) f -> p k f", p=128))
                        inc(i, "dg", 16, e=f"{tag}_wld_{g}")

                @block.tensor
                def _(tensor):
                    idx = 0
                    for g in range(6):
                        wait(tensor, f"{tag}_wld_{g}")
                        if g < 2 or g >= 4:
                            for ml in range(4):
                                for t in range(4):
                                    slot = slots[idx % 4]
                                    if idx >= 4:
                                        wait(tensor, f"{tag}_ev_{idx - 4}")
                                    for k in range(NKD):
                                        lhsT = wqkv_g[:, k * 512 + ml * 128:
                                                      k * 512 + (ml + 1) * 128]
                                        i = tensor.matmul(
                                            slot, lhsT,
                                            xn[k][:, t * 512:(t + 1) * 512],
                                            start=(k == 0), stop=(k == NKD - 1))
                                    inc(i, "tp", 1, e=f"{tag}_p_{idx}")
                                    idx += 1
                        else:
                            for tb in range(NTB):
                                slot = slots[idx % 4]
                                if idx >= 4:
                                    wait(tensor, f"{tag}_ev_{idx - 4}")
                                for k in range(NKD):
                                    i = tensor.matmul(
                                        slot, xn[k][:, tb * 128:(tb + 1) * 128],
                                        wqkv_g[:, k * 512:(k + 1) * 512],
                                        start=(k == 0), stop=(k == NKD - 1))
                                inc(i, "tp", 1, e=f"{tag}_p_{idx}")
                                idx += 1
                        mark(f"{tag}_mm_g{g}", "tp")

                @block.scalar
                def _(scalar):
                    idx = 0
                    for g in range(6):
                        nit = 16
                        for n in range(nit):
                            wait(scalar, f"{tag}_p_{idx}")
                            if g < 4:
                                if idx >= 4:
                                    wait(scalar, f"{tag}_evd_{idx - 4}")
                                i = scalar.copy(qkv_ev[idx % 4], slots[idx % 4])
                            else:
                                ml, t = n // 4, n % 4
                                m = (g - 4) * 4 + ml
                                i = scalar.copy(q_fm[m][:, t * 512:(t + 1) * 512],
                                                slots[idx % 4])
                            inc(i, "sp", 1, e=f"{tag}_ev_{idx}")
                            idx += 1

                @block.sync
                def _(sync):
                    idx = 0
                    for g in range(6):
                        if g < 2:
                            for ml in range(4):
                                m = g * 4 + ml
                                for t in range(4):
                                    wait(sync, f"{tag}_ev_{idx}")
                                    i = sync.dma_start(
                                        out=cc_k_in.ap()[m * 128:(m + 1) * 128,
                                                         t * 512:(t + 1) * 512],
                                        in_=qkv_ev[idx % 4])
                                    inc(i, "dl", 16, e=f"{tag}_evd_{idx}")
                                    idx += 1
                            if g == 1:
                                mark(f"{tag}_kin_done", "dl")
                        elif g < 4:
                            for tb in range(NTB):
                                wait(sync, f"{tag}_ev_{idx}")
                                i = sync.dma_start(
                                    out=cc_v_in.ap()[tb * 128:(tb + 1) * 128,
                                                     (g - 2) * 512:(g - 1) * 512],
                                    in_=qkv_ev[idx % 4])
                                inc(i, "dl", 16, e=f"{tag}_evd_{idx}")
                                idx += 1
                            if g == 3:
                                mark(f"{tag}_vin_done", "dl")
                        else:
                            idx += 16

                @block.gpsimd
                def _(gpsimd):
                    wait(gpsimd, f"{tag}_kin_done")
                    i = gpsimd.collective_compute(
                        "AllGather", ALU.bypass, replica_groups=RG2,
                        ins=[cc_k_in.ap()], outs=[cc_k_out.ap()])
                    inc(i, "cc", 1, e="ag_k")
                    wait(gpsimd, f"{tag}_vin_done")
                    i = gpsimd.collective_compute(
                        "AllGather", ALU.bypass, replica_groups=RG2,
                        ins=[cc_v_in.ap()], outs=[cc_vo.ap()])
                    inc(i, "cc", 1, e="ag_v")

            def emit_attn():
                tag = "at"

                @block.gpsimd
                def _(gpsimd):
                    kbuf = 0
                    for gi in range(8):
                        for c in range(gi + 1):
                            if kbuf < 3:
                                wait(gpsimd, "ag_k")
                                wait(gpsimd, "qkv_mm_g5")
                                wait(gpsimd, "qkv_evd_63")
                            else:
                                wait(gpsimd, f"{tag}_kused_{kbuf - 3}")
                            for h in range(2):
                                i = gpsimd.dma_start(
                                    out=ktile[kbuf % 3].rearrange(
                                        "p (k f) -> p k f",
                                        f=512)[:, :, h * 256:(h + 1) * 256],
                                    in_=cc_k_out.ap()[h * 1024:(h + 1) * 1024,
                                                      2 * c * 128:(2 * c + 2) * 128]
                                    .rearrange("(k p) f -> p k f", p=128))
                                inc(i, "dg", 16)
                            mark(f"{tag}_kld_{kbuf}", "dg")
                            kbuf += 1

                @block.sync
                def _(sync):
                    vbuf = 0
                    for gi in range(8):
                        for blk in range(4 * (gi + 1)):
                            c, sub = blk // 4, blk % 4
                            h, jl = sub // 2, 2 * c + sub % 2
                            if vbuf < 4:
                                wait(sync, "ag_v")
                                wait(sync, "qkv_mm_g5")
                            else:
                                wait(sync, f"{tag}_vused_{vbuf - 4}")
                            i = sync.dma_start(
                                out=vtile[vbuf % 4],
                                in_=cc_vo.ap()[h * 2048 + jl * 128:
                                                  h * 2048 + (jl + 1) * 128, :])
                            inc(i, "dl", 16, e=f"{tag}_vld_{vbuf}")
                            vbuf += 1

                @block.tensor
                def _(tensor):
                    kbuf = 0
                    vbuf = 0
                    ptidx = 0
                    for gi in range(8):
                        for c in range(gi + 1):
                            wait(tensor, f"{tag}_kld_{kbuf}")
                            for qbi in range(2):
                                j = 2 * gi + qbi
                                sq = 2 * kbuf + qbi
                                if sq >= 4:
                                    wait(tensor, f"{tag}_sev_{sq - 4}")
                                for k in range(NKD):
                                    i = tensor.matmul(
                                        slots[sq % 4],
                                        q_fm[k][:, j * 128:(j + 1) * 128],
                                        ktile[kbuf % 3][:, k * 512:(k + 1) * 512],
                                        start=(k == 0), stop=(k == NKD - 1))
                                inc(i, "tp", 1, e=f"{tag}_qk_{sq}")
                            mark(f"{tag}_kused_{kbuf}", "tp")
                            kbuf += 1
                        nblk = 4 * (gi + 1)
                        for blk in range(nblk):
                            wait(tensor, f"{tag}_vld_{vbuf}")
                            if ptidx >= 2:
                                wait(tensor, f"{tag}_ptev_{ptidx - 2}")
                            for qbi in range(2):
                                wait(tensor, f"{tag}_p_{gi}_{qbi}")
                                i = tensor.matmul(
                                    psPT[blk % 2][:, qbi * 128:(qbi + 1) * 128],
                                    prows[qbi][:, blk * 128:(blk + 1) * 128],
                                    ident_f16, start=(qbi == 0), stop=(qbi == 1),
                                    is_transpose=True)
                            inc(i, "tp", 1, e=f"{tag}_pt_{ptidx}")
                            wait(tensor, f"{tag}_ptev_{ptidx}")
                            for qbi in range(2):
                                avps = psB[:, 0:1024] if qbi == 0 else psA[:, 1024:2048]
                                for half in range(2):
                                    i = tensor.matmul(
                                        avps[:, half * 512:(half + 1) * 512],
                                        pt_sb[blk % 2][:,
                                                       qbi * 128:(qbi + 1) * 128],
                                        vtile[vbuf % 4][:,
                                                        half * 512:(half + 1) * 512],
                                        start=(blk == 0), stop=(blk == nblk - 1))
                            inc(i, "tp", 1)
                            mark(f"{tag}_vused_{vbuf}", "tp")
                            vbuf += 1
                            ptidx += 1
                        mark(f"{tag}_av_done_{gi}", "tp")
                        for qbi in range(2):
                            wait(tensor, f"{tag}_avev_{gi}_{qbi}")
                            for dk in range(NKD):
                                bank = psB[:, 0:512] if dk < 4 else psB[:, 512:1024]
                                i = tensor.matmul(
                                    bank[:, (dk % 4) * 128:(dk % 4 + 1) * 128],
                                    av_t[qbi][:, dk * 128:(dk + 1) * 128],
                                    ident_f32, start=(dk % 4 == 0),
                                    stop=(dk % 4 == 3), is_transpose=True)
                            inc(i, "tp", 1, e=f"{tag}_avT_{gi}_{qbi}")
                            wait(tensor, f"{tag}_resadd_{gi}_{qbi}")
                    mark(f"{tag}_done_t", "tp")

                @block.scalar
                def _(scalar):
                    kbuf = 0
                    for gi in range(8):
                        for c in range(gi + 1):
                            for qbi in range(2):
                                sq = 2 * kbuf + qbi
                                wait(scalar, f"{tag}_qk_{sq}")
                                if sq == 0:
                                    wait(scalar, "qkv_mm_g5")
                                i = scalar.activation(
                                    srows[qbi][:, c * 512:(c + 1) * 512],
                                    slots[sq % 4], AF.Identity, bias=0.0,
                                    scale=sb_const[:, 8:9])
                                inc(i, "sp", 1, e=f"{tag}_sev_{sq}")
                            kbuf += 1
                        for qbi in range(2):
                            wait(scalar, f"{tag}_negmax_{gi}_{qbi}")
                            i = scalar.activation(
                                prows[qbi][:, 0:(gi + 1) * 512],
                                srows[qbi][:, 0:(gi + 1) * 512], AF.Exp,
                                bias=(stmp1 if qbi == 0 else stmp2), scale=1.0)
                            inc(i, "sp", 1, e=f"{tag}_exp_{gi}_{qbi}")
                        for qbi in range(2):
                            j = 2 * gi + qbi
                            wait(scalar, f"{tag}_av_done_{gi}")
                            wait(scalar, f"{tag}_rl_{gi}_{qbi}")
                            avps = psB[:, 0:1024] if qbi == 0 else psA[:, 1024:2048]
                            i = scalar.activation(av_t[qbi], avps, AF.Identity,
                                                  bias=0.0, scale=rl[:, j:j + 1])
                            inc(i, "sp", 1, e=f"{tag}_avev_{gi}_{qbi}")

                @block.vector
                def _(vector):
                    ptidx = 0
                    for gi in range(8):
                        last_kbuf = (gi + 1) * (gi + 2) // 2 - 1
                        for qbi in range(2):
                            j = 2 * gi + qbi
                            wait(vector, f"{tag}_sev_{2 * last_kbuf + qbi}")
                            vector.tensor_tensor(
                                srows[qbi][:, gi * 512:(gi + 1) * 512],
                                srows[qbi][:, gi * 512:(gi + 1) * 512],
                                masks[:, qbi * 512:(qbi + 1) * 512], ALU.add)
                            vector.tensor_reduce(
                                (stmp1 if qbi == 0 else stmp2),
                                srows[qbi][:, 0:(gi + 1) * 512], AX.X, ALU.max,
                                negate=True)
                            i = vector.tensor_copy(
                                vdrain[:, 0:1], stmp1 if qbi == 0 else stmp2)
                            inc(i, "vp", 1, e=f"{tag}_negmax_{gi}_{qbi}")
                            wait(vector, f"{tag}_exp_{gi}_{qbi}")
                            vector.tensor_reduce(samax,
                                                 prows[qbi][:, 0:(gi + 1) * 512],
                                                 AX.X, ALU.add)
                            vpad(vector)
                            vpad(vector)
                            vector.reciprocal(rl[:, j:j + 1], samax)
                            vpad(vector)
                            i = vpad(vector)
                            inc(i, "vp", 1, e=f"{tag}_rl_{gi}_{qbi}")
                            mark(f"{tag}_p_{gi}_{qbi}", "vp")
                        for blk in range(4 * (gi + 1)):
                            wait(vector, f"{tag}_pt_{ptidx}")
                            vector.tensor_copy(pt_sb[blk % 2][:, 0:256],
                                               psPT[blk % 2][:, 0:256])
                            i = vector.tensor_copy(
                                vdrain[:, 0:64], pt_sb[blk % 2][:, 192:256])
                            inc(i, "vp", 1, e=f"{tag}_ptev_{ptidx}")
                            ptidx += 1
                        for qbi in range(2):
                            j = 2 * gi + qbi
                            wait(vector, f"{tag}_avT_{gi}_{qbi}")
                            for dk in range(NKD):
                                bank = psB[:, 0:512] if dk < 4 else psB[:, 512:1024]
                                i = vector.tensor_tensor(
                                    x_fm[dk][:, j * 128:(j + 1) * 128],
                                    x_fm[dk][:, j * 128:(j + 1) * 128],
                                    bank[:, (dk % 4) * 128:(dk % 4 + 1) * 128],
                                    ALU.add)
                            inc(i, "vp", 1, e=f"{tag}_resadd_{gi}_{qbi}")
                    mark(f"{tag}_done_v", "vp")

            if DO_FFN1:
                emit_ffn(0, pre=["setup_v"])
            else:
                st["ev"].setdefault("f0_done", ("vp", 0))
            if DO_ATTN:
                emit_qkv()
                emit_attn()
            if DO_FFN2:
                pre2 = (["at_done_t", "at_done_v"] if DO_ATTN else ["f0_done"])
                emit_ffn(1, pre=pre2)

            @block.sync
            def _(sync):
                if DO_FFN2:
                    wait(sync, "f1_done")
                elif DO_ATTN:
                    wait(sync, "at_done_v")
                    wait(sync, "at_done_t")
                elif DO_FFN1:
                    wait(sync, "f0_done")
                base = st["cnt"]["dl"]
                nst = NKD
                for k in range(NKD):
                    i = sync.dma_start(out=outT[k * 128:(k + 1) * 128, :],
                                       in_=x_fm[k])
                    inc(i, "dl", 16)
                if _DEBUG:
                    for src, dst, cols in [(samax, 0, 1), (amax_bc, 1, 1),
                                           (s_q, 2, 1), (inv_q, 3, 1),
                                           (s_q2, 4, 1), (stmp1, 5, 1)]:

                        i = sync.dma_start(out=dbg_scal[:, dst:dst + cols], in_=src, allow_slow_non_contiguous=True)
                        inc(i, "dl", 16)
                        nst += 1
                    for src, dst in [(invs_all, 0), (woamax, 32)]:
                        i = sync.dma_start(out=dbg_row[:, dst:dst + 32], in_=src)
                        inc(i, "dl", 16)
                        nst += 1
                    i = sync.dma_start(out=dbg_xn, in_=xn[0])
                    inc(i, "dl", 16)
                    nst += 1
                    i = sync.dma_start(out=dbg_hid, in_=hidb[0])
                    inc(i, "dl", 16)
                    nst += 1
                    i = sync.dma_start(out=dbg_sbc_d, in_=dbg_sbc_t)
                    inc(i, "dl", 16)
                    nst += 1
                    i = sync.dma_start(out=dbg_sr, in_=srow128)
                    inc(i, "dl", 16)
                    nst += 1
                    for mm_ in range(32):
                        i = sync.dma_start(out=dbg_hd[mm_], in_=hidden[0, mm_])
                        inc(i, "dl", 16)
                        nst += 1
                    i = sync.dma_start(out=dbg_ci, in_=cc_am_in[0].ap())
                    inc(i, "dl", 16)
                    nst += 1
                    i = sync.dma_start(out=dbg_co, in_=cc_am_out[0].ap())
                    inc(i, "dl", 16)
                    nst += 1
                if not st["dry"]:
                    sync.wait_ge(sems["dl"], base + nst * 16)

        st["dry"] = True
        st["cnt"] = {n: 0 for n in sems}
        emit_all(_FakeBlock())
        st["dry"] = False
        st["cnt"] = {n: 0 for n in sems}
        with nc.Block() as block:
            emit_all(block)

    return nc


_NC_CACHE = {}


def _get_nc():
    if "nc" not in _NC_CACHE:
        _NC_CACHE["nc"] = build_nc()
    return _NC_CACHE["nc"]


def kernel(L0_attn_x, L0_attn_W_qkv, L0_ffn_W_in, L0_ffn_b_in, L0_ffn_W_out,
           L0_ffn_b_out):
    x = np.ascontiguousarray(np.asarray(L0_attn_x, dtype=np.float32))
    W_qkv = np.asarray(L0_attn_W_qkv, dtype=np.float32)
    W_in = np.asarray(L0_ffn_W_in, dtype=np.float32)
    b_in = np.asarray(L0_ffn_b_in, dtype=np.float32)
    W_out = np.asarray(L0_ffn_W_out, dtype=np.float32)
    b_out = np.asarray(L0_ffn_b_out, dtype=np.float32)
    B, N, _ = x.shape

    w_in_rs = np.ascontiguousarray(W_in.reshape(D, NG, 512).transpose(1, 0, 2))
    w_qkv_rs = np.ascontiguousarray(
        np.concatenate([W_qkv[:, D:2 * D], W_qkv[:, 2 * D:3 * D], W_qkv[:, 0:D]],
                       axis=1).reshape(D, 6, 512).transpose(1, 0, 2))
    b_in_fm = np.ascontiguousarray(b_in.reshape(32, 128).T)
    b_out_fm = np.ascontiguousarray(b_out.reshape(8, 128).T)
    w_out_c = np.ascontiguousarray(W_out)
    ident32 = np.eye(128, dtype=np.float32)
    ident16 = np.eye(128, dtype=np.float16)
    col = np.arange(512)
    kr = 128 * (2 * ((col % 256) // 128) + col // 256) + col % 128

    in_maps = []
    for c in range(8):
        b, r = c // 2, c % 2
        rows = np.ascontiguousarray(x[b].reshape(32, 128, D)[r::2].reshape(T, D))
        p = np.arange(128)[:, None]
        mask = np.empty((128, 1024), dtype=np.float32)
        for jpar in range(2):
            vis = kr[None, :] <= p + 128 * r + 256 * jpar
            mask[:, jpar * 512:(jpar + 1) * 512] = np.where(vis, 0.0, -30000.0)
        in_maps.append({
            "xT": np.ascontiguousarray(rows.T),
            "w_in_r": w_in_rs,
            "w_out_d": w_out_c,
            "w_qkv_r": w_qkv_rs,
            "b_in_d": b_in_fm,
            "b_outh_d": b_out_fm,
            "roff_d": np.full((128, 1), 128.0 * r, dtype=np.float32),
            "ident16_d": ident16,
            "ident32_d": ident32,
            "maskc_d": mask,
        })

    nc = _get_nc()
    res = run_bass_kernel_spmd(nc, in_maps, core_ids=list(range(8)))

    out = np.empty((B, N, D), dtype=np.float32)
    for c in range(8):
        b, r = c // 2, c % 2
        rows = np.asarray(res.results[c]["outT"]).T
        out[b].reshape(32, 128, D)[r::2] = rows.reshape(NTB, 128, D)
    return out
